# version 14
# baseline (speedup 1.0000x reference)
import contextlib
import numpy as np
import ml_dtypes
import concourse.bass as bass
import concourse.mybir as mybir
from concourse.bass_utils import run_bass_kernel_spmd

F32 = mybir.dt.float32
BF16 = mybir.dt.bfloat16
AF = mybir.ActivationFunctionType
ALU = mybir.AluOpType
AX = mybir.AxisListType

B, S, D, H, HD, II = 2, 4096, 2048, 16, 128, 8192
QR = S // 4
NT = S // 128
NQT = QR // 128
NHC = D // 128
NIC = II // 128
EPS = 1e-5
ESC = float(1.0 / np.sqrt(HD))

ENGS = ("sync", "gpsimd", "scalar", "vector", "tensor")
ENG_SEM = {"sync": "ldA", "gpsimd": "gwA", "scalar": "ss", "vector": "vs", "tensor": "ts"}
DMA_SEMS = ("ldZ", "ldA", "ldB", "ldC", "ldD", "ldE", "ldF", "ldG", "ldH",
            "ldI", "ldJ", "ldK", "ldL",
            "gwA", "gwB", "gwC", "gwD")
SEMS = DMA_SEMS + ("ss", "vs", "ts", "gc")


def _flat(waits):
    out = []
    for t in waits:
        if t is None:
            continue
        if isinstance(t, (list, tuple)) and t and isinstance(t[0], (list, tuple)):
            out.extend(x for x in t if x is not None)
        elif isinstance(t, tuple) and len(t) == 2 and isinstance(t[0], str):
            out.append(t)
        elif isinstance(t, (list, tuple)):
            out.extend(x for x in t if x is not None)
    return out


class Prog:

    def __init__(self):
        self.q = {e: [] for e in ENGS}
        self.cnt = {s: 0 for s in SEMS}

    def op(self, eng, fn, waits=(), inc=True, sem=None):
        if sem is None:
            sem = ENG_SEM[eng]
        amt = 16 if sem in DMA_SEMS else 1
        w = _flat(waits)
        if inc:
            self.cnt[sem] += amt
            tok = (sem, self.cnt[sem])
        else:
            tok = None
        self.q[eng].append(("op", fn, w, sem if inc else None, amt))
        return tok

    def barrier(self, exclude=()):
        snap = {s: v for s, v in self.cnt.items() if s not in exclude}
        for e in ENGS:
            self.q[e].append(("bar", snap))

    def flush(self, block, sems):
        for e in ENGS:
            items = self.q[e]
            if not items:
                continue

            def emit(eng, items=items):
                waited = {}
                for it in items:
                    if it[0] == "bar":
                        for s, v in it[1].items():
                            if v > waited.get(s, 0):
                                eng.wait_ge(sems[s], v)
                                waited[s] = v
                    else:
                        _, fn, w, sem, amt = it
                        for (s, v) in w:
                            if v > waited.get(s, 0):
                                eng.wait_ge(sems[s], v)
                                waited[s] = v
                        inst = fn(eng)
                        if sem is not None:
                            inst.then_inc(sems[sem], amt)

            getattr(block, e)(emit)


def build_nc():
    nc = bass.Bass("TRN2", target_bir_lowering=False, debug=False)
    p = Prog()

    def T(sc, name, shape, dtype, psum=False):
        f = nc.psum_tensor if psum else nc.sbuf_tensor
        return sc.enter_context(f(name, shape, dtype))

    x_d = nc.dram_tensor("x", [S, D], F32, kind="ExternalInput").ap()
    xq_d = nc.dram_tensor("xq", [QR, D], F32, kind="ExternalInput").ap()
    wqT_d = nc.dram_tensor("wqT", [D, 512], F32, kind="ExternalInput").ap()
    wkT_d = nc.dram_tensor("wkT", [D, 512], F32, kind="ExternalInput").ap()
    wvT_d = nc.dram_tensor("wvT", [D, 512], F32, kind="ExternalInput").ap()
    woTs_d = nc.dram_tensor("woTs", [512, D], F32, kind="ExternalInput").ap()
    wg_d = nc.dram_tensor("wg", [D, II], F32, kind="ExternalInput").ap()
    wu_d = nc.dram_tensor("wu", [D, II], F32, kind="ExternalInput").ap()
    wd_d = nc.dram_tensor("wd", [II, D], F32, kind="ExternalInput").ap()
    cos_d = nc.dram_tensor("cosT", [S, 64], F32, kind="ExternalInput").ap()
    sin_d = nc.dram_tensor("sinT", [S, 64], F32, kind="ExternalInput").ap()
    id_d = nc.dram_tensor("ident", [128, 128], BF16, kind="ExternalInput").ap()
    out_d = nc.dram_tensor("out", [QR, D], F32, kind="ExternalOutput").ap()

    s_ln1T = nc.dram_tensor("s_ln1T", [NT, 128, D], BF16, kind="Internal").ap()
    s_kT = nc.dram_tensor("s_kT", [4, 128, S], BF16, kind="Internal").ap()
    s_qT = nc.dram_tensor("s_qT", [4, 128, S], BF16, kind="Internal").ap()
    s_v = nc.dram_tensor("s_v", [2, 128, NT, 256], BF16, kind="Internal").ap()
    rin = [nc.dram_tensor(f"rin{h}", [2048, D], BF16, kind="Internal").ap() for h in range(2)]
    rout = [nc.dram_tensor(f"rout{h}", [512, D], BF16, kind="Internal").ap() for h in range(2)]
    s_x2 = nc.dram_tensor("s_x2", [QR, D], F32, kind="Internal").ap()
    s_ln2T = nc.dram_tensor("s_ln2T", [NQT, 128, D], BF16, kind="Internal").ap()
    s_hT = nc.dram_tensor("s_hT", [NQT, 128, NIC, 128], BF16, kind="Internal").ap()

    with contextlib.ExitStack() as stack:
        sems = {s: stack.enter_context(nc.semaphore(s)) for s in SEMS}

        cpool = stack.enter_context(nc.sbuf_tensor("cpool", [128, 4], F32))
        ident = stack.enter_context(nc.sbuf_tensor("idnt", [128, 128], BF16))
        ones = stack.enter_context(nc.sbuf_tensor("ones", [128, 128], BF16))
        stat = stack.enter_context(nc.sbuf_tensor("stat", [128, 32], F32))

        nc.const_aps.register(F32, float(EPS), cpool[:, 0:1])
        nc.const_aps.register(F32, float(ESC), cpool[:, 1:2])

        block = stack.enter_context(nc.Block())

        t_id = p.op("sync", lambda e: e.dma_start(out=ident[:, :], in_=id_d[:, :]), sem="ldZ")
        p.op("vector", lambda e: e.memset(cpool[:, 0:1], float(EPS)))
        p.op("vector", lambda e: e.memset(cpool[:, 1:2], float(ESC)))
        t_ones = p.op("vector", lambda e: e.memset(ones[:, :], 1.0))

        def ln_tile(xsrc, sqd, dst_bf16, waits, so=0):
            c = lambda k: stat[:, so + k:so + k + 1]
            t1 = p.op("vector", lambda e: e.reduce_sum(c(0), xsrc[:, :], axis=AX.X),
                      waits=waits)
            t2 = p.op("vector", lambda e: e.tensor_scalar_mul(c(0), c(0), 1.0 / D),
                      waits=[t1])
            t3 = p.op("scalar", lambda e: e.activation(out=sqd[:, :], in_=xsrc[:, :], func=AF.Square,
                                                       accum_out=c(1)), waits=waits + [t1])
            t4 = p.op("vector", lambda e: e.tensor_scalar_mul(c(1), c(1), 1.0 / D),
                      waits=[t3])
            t5 = p.op("vector", lambda e: e.tensor_tensor(c(2), c(0), c(0), ALU.mult),
                      waits=[t2])
            t6 = p.op("vector", lambda e: e.tensor_sub(c(2), c(1), c(2)),
                      waits=[t4, t5])
            t7 = p.op("scalar", lambda e: e.activation(out=c(3), in_=c(2), func=AF.Sqrt,
                                                       bias=float(EPS)), waits=[t6])
            t8 = p.op("vector", lambda e: e.reciprocal(c(4), c(3)), waits=[t7])
            t9 = p.op("vector", lambda e: e.tensor_tensor(c(5), c(0), c(4), ALU.mult),
                      waits=[t8])
            t10 = p.op("vector", lambda e: e.tensor_scalar_mul(c(5), c(5), -1.0),
                       waits=[t9])
            ta = p.op("scalar", lambda e: e.activation(out=dst_bf16[:, :], in_=xsrc[:, :], func=AF.Identity,
                                                       bias=c(5), scale=c(4)),
                      waits=[t10])
            return ta

        p12_scope = contextlib.ExitStack()
        cosF = p12_scope.enter_context(nc.sbuf_tensor("cosF", [128, NT * 64], F32))
        sinF = p12_scope.enter_context(nc.sbuf_tensor("sinF", [128, NT * 64], F32))
        wk0_sb = p12_scope.enter_context(nc.sbuf_tensor("wk0_sb", [128, NHC * 512], BF16))
        wv0_sb = p12_scope.enter_context(nc.sbuf_tensor("wv0_sb", [128, NHC * 512], BF16))
        wq0_sb = p12_scope.enter_context(nc.sbuf_tensor("wq0_sb", [128, NHC * 512], BF16))
        wq_view = wqT_d.rearrange("(hc p) d -> p hc d", p=128)
        wk_view = wkT_d.rearrange("(hc p) d -> p hc d", p=128)
        wv_view = wvT_d.rearrange("(hc p) d -> p hc d", p=128)

        def wsbv(t):
            return t[:, :].rearrange("p (hc c) -> p hc c", c=512)

        tbl_toks = [
            p.op("gpsimd", lambda e: e.dma_start(
                out=cosF[:, :].rearrange("p (t z) -> p t z", z=64),
                in_=cos_d.rearrange("(t p) z -> p t z", p=128)), sem="gwD"),
            p.op("gpsimd", lambda e: e.dma_start(
                out=sinF[:, :].rearrange("p (t z) -> p t z", z=64),
                in_=sin_d.rearrange("(t p) z -> p t z", p=128)), sem="gwD"),
        ]
        t_tbl = tbl_toks[-1]
        t_wk0 = p.op("gpsimd", lambda e: e.dma_start(
            out=wsbv(wk0_sb), in_=wk_view[:, :, 0:512]), waits=[], sem="gwA")
        t_wv0 = p.op("gpsimd", lambda e: e.dma_start(
            out=wsbv(wv0_sb), in_=wv_view[:, :, 0:512]), waits=[], sem="gwB")
        t_wq0 = p.op("gpsimd", lambda e: e.dma_start(
            out=wsbv(wq0_sb), in_=wq_view[:, :, 0:512]), waits=[], sem="gwC")
        with contextlib.ExitStack() as sc:
            NP1 = 6
            xin = [T(sc, f"xin{i}", [128, D], F32) for i in range(NP1)]
            sqd = [T(sc, f"sqd{i}", [128, D], F32) for i in range(2)]
            ln_sb = [T(sc, f"lnsb{i}", [128, D], BF16) for i in range(NP1)]
            lnT_sb = [T(sc, f"lnT{i}", [128, D], BF16) for i in range(NP1)]
            psTl = [T(sc, f"psTl{i}", [128, 1024], BF16, psum=True) for i in range(2)]

            xin_free = [None] * NP1
            lnapply_war = [None] * NP1
            lnT_free = [None] * NP1
            prev_ln = [None] * NP1
            prev_stat = [None] * 4
            sq_war = [None, None]
            psTl_free = [None, None]

            def p1_a(i, src_rows):
                sl = i % NP1
                tl = p.op("sync", lambda e, sl=sl: e.dma_start(out=xin[sl][:, :], in_=src_rows),
                          waits=[xin_free[sl]], sem=("ldA", "ldB", "ldC", "ldD", "ldE", "ldF")[sl])
                return {"sl": sl, "so": 8 * (i % 4), "tl": tl}

            def p1_b(st):
                sl, so = st["sl"], st["so"]
                c = lambda k: stat[:, so + k:so + k + 1]
                st["c"] = c
                w0 = [st["tl"], prev_ln[sl], prev_stat[(st["so"] // 8) % 4], lnapply_war[sl]]
                st["t1"] = p.op("vector", lambda e, sl=sl, c=c: e.reduce_sum(
                    c(0), xin[sl][:, :], axis=AX.X), waits=w0)
                st["t3"] = p.op("scalar", lambda e, sl=sl, c=c: e.activation(
                    out=sqd[sl % 2][:, :], in_=xin[sl][:, :], func=AF.Square, accum_out=c(1)),
                    waits=w0 + [sq_war[sl % 2]])

            def p1_c(st):
                sl, c = st["sl"], st["c"]
                t2 = p.op("vector", lambda e, c=c: e.tensor_scalar_mul(c(0), c(0), 1.0 / D),
                          waits=[st["t1"]])
                t4 = p.op("vector", lambda e, c=c: e.tensor_scalar_mul(c(1), c(1), 1.0 / D),
                          waits=[st["t3"]])
                t5 = p.op("vector", lambda e, c=c: e.tensor_tensor(c(2), c(0), c(0), ALU.mult),
                          waits=[t2])
                t6 = p.op("vector", lambda e, c=c: e.tensor_sub(c(2), c(1), c(2)),
                          waits=[t4, t5])
                t7 = p.op("scalar", lambda e, c=c: e.activation(out=c(3), in_=c(2), func=AF.Sqrt,
                                                                bias=float(EPS)), waits=[t6])
                t8 = p.op("vector", lambda e, c=c: e.reciprocal(c(4), c(3)), waits=[t7])
                t9 = p.op("vector", lambda e, c=c: e.tensor_tensor(c(5), c(0), c(4), ALU.mult),
                          waits=[t8])
                t10 = p.op("vector", lambda e, c=c: e.tensor_scalar_mul(c(5), c(5), -1.0),
                           waits=[t9])
                ta = p.op("scalar", lambda e, sl=sl, c=c: e.activation(
                    out=ln_sb[sl][:, :], in_=xin[sl][:, :], func=AF.Identity,
                    bias=c(5), scale=c(4)), waits=[t10])
                prev_ln[sl] = ta
                prev_stat[(st["so"] // 8) % 4] = ta
                xin_free[sl] = ta
                sq_war[sl % 2] = ta
                st["ta"] = ta

            def p1_d(st, dst_block):
                sl, ta = st["sl"], st["ta"]
                pl = sl % 2
                ev_toks = []
                last_tt = None
                for g in range(2):
                    tts = None
                    for k in range(8):
                        hc = g * 8 + k
                        tts = p.op("tensor", lambda e, hc=hc, k=k, sl=sl, pl=pl: e.transpose(
                            psTl[pl][:, k * 128:(k + 1) * 128],
                            ln_sb[sl][:, hc * 128:(hc + 1) * 128], ident[:, :]),
                            waits=([ta, t_id, psTl_free[pl]] if k == 0 else []),
                            inc=(k == 7))
                    last_tt = tts
                    tev = p.op("vector", lambda e, g=g, sl=sl, pl=pl: e.tensor_copy(
                        lnT_sb[sl][:, g * 1024:(g + 1) * 1024], psTl[pl][:, :]),
                        waits=[tts, lnT_free[sl]])
                    psTl_free[pl] = tev
                    ev_toks.append(tev)
                lnapply_war[sl] = last_tt
                tw = p.op("scalar", lambda e, sl=sl: e.dma_start(out=dst_block, in_=lnT_sb[sl][:, :]),
                          waits=ev_toks, sem=("ldI", "ldJ", "ldK", "ldL", "ldG", "ldH")[sl])
                lnT_free[sl] = tw
                return tw

            srcs = [(x_d[t * 128:(t + 1) * 128, :], s_ln1T[t, :, :]) for t in range(NT)]
            states = {}
            NTOT = len(srcs)
            for i in range(NTOT + 4):
                if i < NTOT:
                    states[i] = p1_a(i, srcs[i][0])
                if 0 <= i - 2 < NTOT:
                    p1_b(states[i - 2])
                if 0 <= i - 3 < NTOT:
                    p1_c(states[i - 3])
                if 0 <= i - 4 < NTOT:
                    p1_d(states[i - 4], srcs[i - 4][1])
                    del states[i - 4]

        p.barrier()

        with contextlib.ExitStack() as sc:
            lnP2 = [T(sc, f"p2lnT{i}", [128, D], BF16) for i in range(2)]
            rA = T(sc, "rA", [128, 256], F32)
            rB = T(sc, "rB", [128, 256], F32)
            rC = T(sc, "rC", [128, 256], F32)
            rD = T(sc, "rD", [128, 256], F32)
            K_sb = [T(sc, f"K_sb{i}", [128, 512], BF16) for i in range(2)]
            Q_sb = [T(sc, f"Q_sb{i}", [128, 512], BF16) for i in range(2)]
            V_sb = [T(sc, f"V_sb{i}", [128, 512], BF16) for i in range(2)]
            kT_st = T(sc, "kT_st", [128, 2048], BF16)
            qT_st = T(sc, "qT_st", [128, 2048], BF16)
            psK = [T(sc, f"psK{i}", [128, 512], F32, psum=True) for i in range(2)]
            psV = [T(sc, f"psV{i}", [128, 512], F32, psum=True) for i in range(2)]
            psQ = [T(sc, f"psQ{i}", [128, 512], F32, psum=True) for i in range(2)]
            psTk = T(sc, "psTk", [128, 1024], BF16, psum=True)

            lnP2_free = [None, None]
            psK_free = [None, None]
            psV_free = [None, None]
            psQ_free = [None, None]
            rope_busy = None
            k_free = [None, None]
            q_free = [None, None]
            v_free = [None, None]
            kst_free = None
            qst_free = None
            psTk_free = None
            pend = [None]

            def rope_emit(ps, idx, out_sb, waits):
                nonlocal rope_busy
                pr = ps[:, :].rearrange("p (h two z) -> p h two z", two=2, z=64)
                re_, im_ = pr[:, :, 0, :], pr[:, :, 1, :]
                ct_ap = cosF[:, :].rearrange("p (t z) -> p t z", z=64)[:, idx, :]
                st_ap = sinF[:, :].rearrange("p (t z) -> p t z", z=64)[:, idx, :]
                cb = ct_ap.unsqueeze(1).broadcast_to([128, 4, 64])
                sb = st_ap.unsqueeze(1).broadcast_to([128, 4, 64])
                rAv = rA[:, :].rearrange("p (h z) -> p h z", z=64)
                rBv = rB[:, :].rearrange("p (h z) -> p h z", z=64)
                rCv = rC[:, :].rearrange("p (h z) -> p h z", z=64)
                rDv = rD[:, :].rearrange("p (h z) -> p h z", z=64)
                ov = out_sb[:, :].rearrange("p (h two z) -> p h two z", two=2, z=64)
                w0 = list(waits) + [rope_busy, t_tbl]
                t1 = p.op("vector", lambda e: e.tensor_tensor(rAv, re_, cb, ALU.mult), waits=w0)
                t2 = p.op("vector", lambda e: e.tensor_tensor(rBv, im_, sb, ALU.mult), waits=[])
                t3 = p.op("vector", lambda e: e.tensor_tensor(rCv, re_, sb, ALU.mult), waits=[])
                t4 = p.op("vector", lambda e: e.tensor_tensor(rDv, im_, cb, ALU.mult), waits=[])
                t5 = p.op("vector", lambda e: e.tensor_tensor(ov[:, :, 0, :], rAv, rBv, ALU.subtract),
                          waits=[t1, t2])
                t6 = p.op("vector", lambda e: e.tensor_tensor(ov[:, :, 1, :], rCv, rDv, ALU.add),
                          waits=[t3, t4])
                rope_busy = t6
                return t6

            def mm_group(ps, wt, t_w, sl, tln, free_tok):
                tk = None
                for hc in range(NHC):
                    tk = p.op("tensor", lambda e, hc=hc, sl=sl, ps=ps, wt=wt: e.matmul(
                        ps[sl][:, :], lnP2[sl][:, hc * 128:(hc + 1) * 128],
                        wt[:, hc * 512:(hc + 1) * 512],
                        start=(hc == 0), stop=(hc == NHC - 1)),
                        waits=([tln, t_w, free_tok] if hc == 0 else []),
                        inc=(hc == NHC - 1))
                return tk

            for kt in range(NT):
                sl = kt % 2
                tln = p.op("sync", lambda e, kt=kt, sl=sl: e.dma_start(
                    out=lnP2[sl][:, :], in_=s_ln1T[kt, :, :]), waits=[lnP2_free[sl]],
                    sem=("ldA", "ldB")[sl])
                tk = mm_group(psK, wk0_sb, t_wk0, sl, tln, psK_free[sl])
                tv = mm_group(psV, wv0_sb, t_wv0, sl, tln, psV_free[sl])
                tq = mm_group(psQ, wq0_sb, t_wq0, sl, tln, psQ_free[sl])
                lnP2_free[sl] = tq
                trk = rope_emit(psK[sl], kt, K_sb[sl], [tk, k_free[sl]])
                psK_free[sl] = trk
                tve = p.op("vector", lambda e, sl=sl: e.tensor_copy(V_sb[sl][:, :], psV[sl][:, :]),
                           waits=[tv, v_free[sl]])
                psV_free[sl] = tve
                vw = []
                for vp in range(2):
                    vw.append(p.op("scalar", lambda e, vp=vp, sl=sl, kt=kt: e.dma_start(
                        out=s_v[vp, :, kt, :],
                        in_=V_sb[sl][:, vp * 256:(vp + 1) * 256]), waits=[tve],
                        sem=("ldG", "ldH")[sl]))
                v_free[sl] = vw[-1]
                trq = rope_emit(psQ[sl], kt, Q_sb[sl], [tq, q_free[sl]])
                psQ_free[sl] = trq
                if pend[0] is not None:
                    pend[0]()
                    pend[0] = None

                def post(kt=kt, sl=sl, trk=trk, trq=trq):
                    nonlocal psTk_free, kst_free, qst_free
                    w_i = kt % 4
                    tts = None
                    for hs in range(4):
                        tts = p.op("tensor", lambda e, hs=hs, sl=sl: e.transpose(
                            psTk[:, hs * 128:(hs + 1) * 128],
                            K_sb[sl][:, hs * 128:(hs + 1) * 128], ident[:, :]),
                            waits=([trk, psTk_free] if hs == 0 else []), inc=False)
                    for hs in range(4):
                        tts = p.op("tensor", lambda e, hs=hs, sl=sl: e.transpose(
                            psTk[:, 512 + hs * 128:512 + (hs + 1) * 128],
                            Q_sb[sl][:, hs * 128:(hs + 1) * 128], ident[:, :]),
                            waits=([trq] if hs == 0 else []), inc=(hs == 3))
                    k_free[sl] = tts
                    q_free[sl] = tts
                    tev = p.op("vector", lambda e, w_i=w_i: e.tensor_copy(
                        kT_st[:, :].rearrange("p (hs w c) -> p hs w c", w=4, c=128)[:, :, w_i, :],
                        psTk[:, 0:512].rearrange("p (hs c) -> p hs c", c=128)),
                        waits=[tts, kst_free])
                    tev2 = p.op("vector", lambda e, w_i=w_i: e.tensor_copy(
                        qT_st[:, :].rearrange("p (hs w c) -> p hs w c", w=4, c=128)[:, :, w_i, :],
                        psTk[:, 512:1024].rearrange("p (hs c) -> p hs c", c=128)),
                        waits=[qst_free])
                    psTk_free = tev2
                    if w_i == 3:
                        fl = []
                        for hs in range(4):
                            fl.append(p.op("scalar", lambda e, hs=hs, kt=kt: e.dma_start(
                                out=s_kT[hs, :, (kt - 3) * 128:(kt + 1) * 128],
                                in_=kT_st[:, hs * 512:(hs + 1) * 512]), waits=[tev],
                                sem="ldC"))
                        kst_free = fl[-1]
                        fl2 = []
                        for hs in range(4):
                            fl2.append(p.op("scalar", lambda e, hs=hs, kt=kt: e.dma_start(
                                out=s_qT[hs, :, (kt - 3) * 128:(kt + 1) * 128],
                                in_=qT_st[:, hs * 512:(hs + 1) * 512]), waits=[tev2],
                                sem="ldD"))
                        qst_free = fl2[-1]

                pend[0] = post
            if pend[0] is not None:
                pend[0]()
                pend[0] = None

        p12_scope.close()
        p.barrier()


        p45_scope = contextlib.ExitStack()
        wgs0 = p45_scope.enter_context(nc.sbuf_tensor("wgs0", [128, NHC * 1024], BF16))
        wus0_hold = [p45_scope.enter_context(nc.sbuf_tensor("wus0", [128, NHC * 1024], BF16))]
        wg_view = wg_d.rearrange("(hc p) i -> p hc i", p=128)
        wu_view = wu_d.rearrange("(hc p) i -> p hc i", p=128)
        twg0_holder = [None, None]
        RG = [[0, 1, 2, 3], [4, 5, 6, 7]]
        o_scope = contextlib.ExitStack()
        woTs_sb = o_scope.enter_context(nc.sbuf_tensor("woTs_sb", [128, 4 * D], BF16))
        t_wo = p.op("gpsimd", lambda e: e.dma_start(
            out=woTs_sb[:, :].rearrange("p (c o) -> p c o", o=D),
            in_=woTs_d.rearrange("(c p) o -> p c o", p=128)), waits=[], sem="gwD")
        rs_tok = [None, None]
        with contextlib.ExitStack() as sc:
            KA = T(sc, "KA", [128, 4 * S], BF16)
            QA = T(sc, "QA", [128, 4 * S], BF16)
            VA = T(sc, "VA", [128, NT * 512], BF16)
            O_st = [T(sc, f"Ost{i}", [128, 4 * 512], BF16) for i in range(2)]
            ap_sb = [T(sc, f"apsb{i}", [128, D], BF16) for i in range(2)]
            expT = [T(sc, f"expT{i}", [128, 1024], BF16) for i in range(2)]
            rs = T(sc, "rs", [128, 512], F32)
            psS = [T(sc, f"psS{i}", [128, 1024], F32, psum=True) for i in range(2)]
            psSum = [T(sc, f"psSum{i}", [128, 512], F32, psum=True) for i in range(2)]
            psO = [T(sc, f"psO{i}", [128, 512], F32, psum=True) for i in range(2)]

            tK = [p.op("sync", lambda e, h=h: e.dma_start(
                out=KA[:, h * S:(h + 1) * S], in_=s_kT[h, :, :]),
                sem=("ldA", "ldB", "ldC", "ldD")[h]) for h in range(4)]
            tQ = [p.op("sync", lambda e, h=h: e.dma_start(
                out=QA[:, h * S:(h + 1) * S], in_=s_qT[h, :, :]),
                sem=("ldE", "ldF", "ldG", "ldH")[h]) for h in range(4)]
            tV = [p.op("sync", lambda e, vp=vp: e.dma_start(
                out=VA[:, :].rearrange("p (kc g c) -> p kc g c", g=2, c=256)[:, :, vp, :],
                in_=s_v[vp, :, :, :]), sem=("ldI", "ldJ")[vp]) for vp in range(2)]

            psS_free = [None, None]
            expT_free = [None] * 2
            psSum_free = [None, None]
            psO_free = [None, None]
            rs_free = None
            O_free = [[], []]
            ap_free = [[], []]
            store_half = {0: [], 1: []}
            qidx = 0

            qt_order = [0, 2, 4, 6, 1, 3, 5, 7]
            for n_q, qt in enumerate(qt_order):
                osl_q = n_q % 2
                o_done = []
                for h in range(4):
                    osl = qidx % 2
                    qidx += 1
                    tload = [tK[h], tQ[h], tV[h // 2]]

                    def s_mm(kp):
                        tS = None
                        for half in range(2):
                            kc = 2 * kp + half
                            ss_ = kp % 2
                            tS = p.op("tensor", lambda e, kc=kc, half=half, ss_=ss_, h=h, qt=qt: e.matmul(
                                psS[ss_][:, half * 512:(half + 1) * 512],
                                KA[:, h * S + kc * 128:h * S + (kc + 1) * 128],
                                QA[:, h * S + qt * 512:h * S + (qt + 1) * 512],
                                start=True, stop=True),
                                waits=(tload + [psS_free[ss_]] if half == 0 else []),
                                inc=(half == 1))
                        return tS

                    tS_pend = {0: s_mm(0), 1: s_mm(1)}
                    tsum = tpv = None
                    for kp in range(16):
                        es = kp % 2
                        ss_ = kp % 2
                        texp = p.op("scalar", lambda e, ss_=ss_, es=es: e.activation(
                            out=expT[es][:, :], in_=psS[ss_][:, :], func=AF.Exp,
                            scale=float(ESC)), waits=[tS_pend.pop(kp), expT_free[es]])
                        psS_free[ss_] = texp
                        for half in range(2):
                            tsum = p.op("tensor", lambda e, es=es, half=half, kp=kp, osl=osl: e.matmul(
                                psSum[osl][:, :], ones[:, :], expT[es][:, half * 512:(half + 1) * 512],
                                start=(kp == 0 and half == 0), stop=(kp == 15 and half == 1)),
                                waits=(([texp, t_ones] + ([psSum_free[osl]] if kp == 0 else []))
                                       if half == 0 else []),
                                inc=(kp == 15 and half == 1))
                        for half in range(2):
                            kc = 2 * kp + half
                            tpv = p.op("tensor", lambda e, kc=kc, es=es, half=half, h=h, kp=kp, osl=osl: e.matmul(
                                psO[osl][:, :],
                                VA[:, kc * 512 + (h // 2) * 256 + (h % 2) * 128:
                                    kc * 512 + (h // 2) * 256 + (h % 2) * 128 + 128],
                                expT[es][:, half * 512:(half + 1) * 512],
                                start=(kp == 0 and half == 0), stop=(kp == 15 and half == 1)),
                                waits=(([texp] + ([psO_free[osl]] if kp == 0 else []))
                                       if half == 0 else []),
                                inc=(kp == 15 and half == 1))
                        expT_free[es] = tpv
                        if kp + 2 < 16:
                            tS_pend[kp + 2] = s_mm(kp + 2)
                    trc = p.op("vector", lambda e, osl=osl: e.reciprocal(rs[:, :], psSum[osl][:, :]),
                               waits=[tsum, rs_free])
                    psSum_free[osl] = trc
                    tno = p.op("vector", lambda e, h=h, osl=osl, osl_q=osl_q: e.tensor_tensor(
                        O_st[osl_q][:, h * 512:(h + 1) * 512],
                        psO[osl][:, :], rs[:, :], ALU.mult),
                        waits=[tpv, trc] + (O_free[osl_q] if h == 0 else []))
                    psO_free[osl] = tno
                    rs_free = tno
                    o_done.append(tno)
                half = qt % 2
                j = qt // 2
                p35_last = []
                for st_i in range(4):
                    asl = (n_q * 4 + st_i) % 2
                    ev_toks = []
                    for oc in range(4):
                        gslot = (st_i * 4 + oc) % 4
                        ps_ap = psS[gslot // 2][:, (gslot % 2) * 512:(gslot % 2 + 1) * 512]
                        tw = None
                        for hl in range(4):
                            tw = p.op("tensor", lambda e, hl=hl, st_i=st_i, ps_ap=ps_ap, osl_q=osl_q, oc=oc: e.matmul(
                                ps_ap,
                                O_st[osl_q][:, hl * 512 + st_i * 128:hl * 512 + (st_i + 1) * 128],
                                woTs_sb[:, hl * D + oc * 512:hl * D + (oc + 1) * 512],
                                start=(hl == 0), stop=(hl == 3)),
                                waits=([t_wo, psS_free[gslot // 2], o_done[-1]]
                                       + (ap_free[asl] if oc == 0 else []) if hl == 0 else []),
                                inc=(hl == 3))
                        tev = p.op("vector", lambda e, oc=oc, asl=asl, ps_ap=ps_ap: e.tensor_copy(
                            ap_sb[asl][:, oc * 512:(oc + 1) * 512], ps_ap),
                            waits=[tw])
                        psS_free[gslot // 2] = tev
                        ev_toks.append(tev)
                    row0 = j * 512 + st_i * 128
                    tst = p.op("scalar", lambda e, asl=asl, half=half, row0=row0: e.dma_start(
                        out=rin[half][row0:row0 + 128, :], in_=ap_sb[asl][:, :]),
                        waits=ev_toks, sem=("ldK", "ldL")[asl])
                    ap_free[asl] = [tst]
                    store_half[half].append(tst)
                    p35_last.append(tst)
                O_free[osl_q] = [p35_last[-1]]
                if n_q == 0:
                    twg0_holder[0] = p.op("gpsimd", lambda e: e.dma_start(
                        out=wgs0[:, :].rearrange("p (hc c) -> p hc c", c=1024),
                        in_=wg_view[:, :, 0:1024]), waits=[p35_last[-1]], sem="gwB")
                    twg0_holder[1] = p.op("gpsimd", lambda e: e.dma_start(
                        out=wus0_hold[0][:, :].rearrange("p (hc c) -> p hc c", c=1024),
                        in_=wu_view[:, :, 0:1024]), waits=[], sem="gwC")
                if n_q == 3:
                    rs_tok[0] = p.op("gpsimd", lambda e: e.collective_compute(
                        "ReduceScatter", ALU.add, replica_groups=RG,
                        ins=[rin[0][:, :]], outs=[rout[0][:, :]]),
                        waits=list(store_half[0]), sem="gc")
            rs2_waits = list(store_half[1])

        o_scope.close()
        p.barrier(exclude=("gc",))

        p.barrier(exclude=("gc",))

        with contextlib.ExitStack() as sc:
            wgs = [wgs0, T(sc, "wgs1", [128, NHC * 1024], BF16)]
            wus = [wus0_hold[0], T(sc, "wus1", [128, NHC * 1024], BF16)]
            tw_ig1 = (
                p.op("gpsimd", lambda e: e.dma_start(
                    out=wgs[1][:, :].rearrange("p (hc c) -> p hc c", c=1024),
                    in_=wg_view[:, :, 1024:2048]), waits=[], sem="gwA"),
                p.op("gpsimd", lambda e: e.dma_start(
                    out=wus[1][:, :].rearrange("p (hc c) -> p hc c", c=1024),
                    in_=wu_view[:, :, 1024:2048]), waits=[], sem="gwD"),
            )
            rs_tok[1] = p.op("gpsimd", lambda e: e.collective_compute(
                "ReduceScatter", ALU.add, replica_groups=RG,
                ins=[rin[1][:, :]], outs=[rout[1][:, :]]),
                waits=rs2_waits, sem="gc")
            at_t = [T(sc, f"att{i}", [128, D], BF16) for i in range(2)]
            x_t = [T(sc, f"xt{i}", [128, D], F32) for i in range(2)]
            x2_sb = T(sc, "x2sb", [128, D], F32)
            ln2_sb = [T(sc, f"ln2sb{i}", [128, D], BF16) for i in range(2)]
            ln2T_sb = T(sc, "ln2T_sb", [128, D], BF16)
            psT2 = T(sc, "psT2", [128, 1024], BF16, psum=True)

            at_free = [None, None]
            xt_free = [None, None]
            x2_free = []
            ln2_war = [None, None]
            ln2T_free = None
            psT2_free = None
            ln2war2 = None
            pend4 = [None]
            tl2_tok = {}

            def p4_tile(qt):
                sl = qt % 2
                ta_ld = p.op("sync", lambda e, qt=qt, sl=sl: e.dma_start(
                    out=at_t[sl][:, :],
                    in_=rout[qt // 4][(qt % 4) * 128:(qt % 4 + 1) * 128, :]),
                    waits=[at_free[sl], rs_tok[qt // 4]], sem=("ldE", "ldF")[sl])
                tx = p.op("sync", lambda e, qt=qt, sl=sl: e.dma_start(
                    out=x_t[sl][:, :], in_=xq_d[qt * 128:(qt + 1) * 128, :]), waits=[xt_free[sl]],
                    sem=("ldG", "ldH")[sl])
                tev = p.op("vector", lambda e, sl=sl: e.tensor_tensor(
                    x2_sb[:, :], x_t[sl][:, :], at_t[sl][:, :], ALU.add),
                    waits=[ta_ld, tx] + x2_free)
                at_free[sl] = tev
                tst = p.op("scalar", lambda e, qt=qt: e.dma_start(
                    out=s_x2[qt * 128:(qt + 1) * 128, :], in_=x2_sb[:, :]), waits=[tev],
                    sem=("ldI", "ldJ")[sl])
                ta = ln_tile(x2_sb, x_t[sl], ln2_sb[sl], [tev, ln2_war[sl], ln2war2],
                             so=8 * sl)
                ln2_war[sl] = ta
                xt_free[sl] = ta
                x2_free[:] = [tst, ta]
                if pend4[0] is not None:
                    pend4[0]()
                    pend4[0] = None

                def post4(qt=qt, sl=sl, ta=ta):
                    nonlocal ln2war2, psT2_free, ln2T_free
                    ev2 = []
                    last_tt = None
                    for g in range(2):
                        tts = None
                        for k in range(8):
                            hc = g * 8 + k
                            tts = p.op("tensor", lambda e, hc=hc, k=k, sl=sl: e.transpose(
                                psT2[:, k * 128:(k + 1) * 128],
                                ln2_sb[sl][:, hc * 128:(hc + 1) * 128], ident[:, :]),
                                waits=([ta, psT2_free] if k == 0 else []), inc=(k == 7))
                        last_tt = tts
                        tev2 = p.op("vector", lambda e, g=g: e.tensor_copy(
                            ln2T_sb[:, g * 1024:(g + 1) * 1024], psT2[:, :]),
                            waits=[tts, ln2T_free])
                        psT2_free = tev2
                        ev2.append(tev2)
                    ln2war2 = last_tt
                    tl2 = p.op("scalar", lambda e, qt=qt: e.dma_start(
                        out=s_ln2T[qt, :, :], in_=ln2T_sb[:, :]), waits=ev2, sem="ldK")
                    ln2T_free = tl2
                    tl2_tok[qt] = tl2

                pend4[0] = post4
            l2t = [T(sc, f"l2t{i}", [128, D], BF16) for i in range(2)]
            g_f = T(sc, "g_f", [128, 1024], F32)
            h_f = T(sc, "h_f", [128, 1024], F32)
            a_f = T(sc, "a_f", [128, 1024], F32)
            hm = [T(sc, f"hm{i}", [128, 1024], BF16) for i in range(2)]
            mw = {nm: T(sc, f"mw_{nm}", [128, 256], F32)
                  for nm in ("w01", "w02", "w03", "w12", "w13", "w23", "t0")}
            st5 = T(sc, "st5", [128, 1024], BF16)
            psG = [T(sc, f"psG{i}", [128, 1024], F32, psum=True) for i in range(2)]
            psU = T(sc, "psU", [128, 1024], F32, psum=True)
            psT5 = T(sc, "psT5", [128, 1024], BF16, psum=True)

            wg_free = [None, None]
            l2t_free = [None, None]
            psG_free = [None, None]
            psU_free = None
            psT5_free = None
            g_free = None
            h_free = None
            a_free = None
            hm_free = [None, None]
            st5_free = None
            mask_busy = None
            pend5 = [None]

            for ig in range(8):
                wsl = ig % 2
                isl = slice(ig * 1024, (ig + 1) * 1024)
                if ig == 0:
                    twg, twu = twg0_holder
                    for _q in range(3):
                        p4_tile(_q)
                elif ig == 1:
                    twg, twu = tw_ig1
                else:
                    twg = p.op("gpsimd", lambda e, s=isl, wsl=wsl: e.dma_start(
                        out=wgs[wsl][:, :].rearrange("p (hc c) -> p hc c", c=1024),
                        in_=wg_view[:, :, s]), waits=[wg_free[wsl]], sem=("gwB", "gwA")[wsl])
                    twu = p.op("gpsimd", lambda e, s=isl, wsl=wsl: e.dma_start(
                        out=wus[wsl][:, :].rearrange("p (hc c) -> p hc c", c=1024),
                        in_=wu_view[:, :, s]), waits=[], sem=("gwC", "gwD")[wsl])
                last_mm = None
                for rt in range(NQT):
                    if ig == 0:
                        if rt + 3 < NQT:
                            p4_tile(rt + 3)
                        elif rt + 3 == NQT and pend4[0] is not None:
                            pend4[0]()
                            pend4[0] = None
                    lsl = rt % 2
                    gsl = rt % 2
                    hsl = rt % 2
                    tl2 = p.op("sync", lambda e, rt=rt, lsl=lsl: e.dma_start(
                        out=l2t[lsl][:, :], in_=s_ln2T[rt, :, :]),
                        waits=[l2t_free[lsl], tl2_tok.get(rt)],
                        sem=("ldA", "ldB")[lsl])
                    tg = tu = None
                    for hc in range(NHC):
                        for half in range(2):
                            tg = p.op("tensor", lambda e, hc=hc, half=half, wsl=wsl, lsl=lsl, gsl=gsl: e.matmul(
                                psG[gsl][:, half * 512:(half + 1) * 512],
                                l2t[lsl][:, hc * 128:(hc + 1) * 128],
                                wgs[wsl][:, hc * 1024 + half * 512:hc * 1024 + (half + 1) * 512],
                                start=(hc == 0), stop=(hc == NHC - 1)),
                                waits=([tl2, twg, psG_free[gsl]] if (hc == 0 and half == 0) else []),
                                inc=(hc == NHC - 1 and half == 1))
                    for hc in range(NHC):
                        for half in range(2):
                            tu = p.op("tensor", lambda e, hc=hc, half=half, wsl=wsl, lsl=lsl: e.matmul(
                                psU[:, half * 512:(half + 1) * 512],
                                l2t[lsl][:, hc * 128:(hc + 1) * 128],
                                wus[wsl][:, hc * 1024 + half * 512:hc * 1024 + (half + 1) * 512],
                                start=(hc == 0), stop=(hc == NHC - 1)),
                                waits=([tl2, twu, psU_free] if (hc == 0 and half == 0) else []),
                                inc=(hc == NHC - 1 and half == 1))
                    last_mm = tu
                    l2t_free[lsl] = tu
                    if pend5[0] is not None:
                        pend5[0]()
                        pend5[0] = None
                    tsi = p.op("scalar", lambda e, gsl=gsl: e.activation(out=g_f[:, :], in_=psG[gsl][:, :],
                                                                         func=AF.Silu), waits=[tg, g_free])
                    psG_free[gsl] = tsi
                    th = p.op("vector", lambda e: e.tensor_tensor(h_f[:, :], g_f[:, :], psU[:, :],
                                                                  ALU.mult), waits=[tsi, tu, h_free])
                    psU_free = th
                    g_free = th
                    tab = p.op("scalar", lambda e: e.activation(out=a_f[:, :], in_=h_f[:, :],
                                                                func=AF.Abs), waits=[th, a_free])
                    qv = [a_f[:, i * 256:(i + 1) * 256] for i in range(4)]
                    pairs = [("w01", 0, 1), ("w02", 0, 2), ("w03", 0, 3),
                             ("w12", 1, 2), ("w13", 1, 3), ("w23", 2, 3)]
                    cmp_toks = {}
                    tprev = mask_busy
                    for nm, i, j in pairs:
                        cmp_toks[nm] = p.op("vector", lambda e, nm=nm, i=i, j=j: e.tensor_tensor(
                            mw[nm][:, :], qv[i], qv[j], ALU.is_ge), waits=[tab, tprev])
                        tprev = None
                    hm_toks = []

                    def mfin(expr_tok, thr, op, qidx, hsl=hsl):
                        tap = p.op("vector", lambda e, thr=thr, op=op, qidx=qidx, hsl=hsl: e.scalar_tensor_tensor(
                            hm[hsl][:, qidx * 256:(qidx + 1) * 256],
                            mw["t0"][:, :], thr, h_f[:, qidx * 256:(qidx + 1) * 256],
                            op, ALU.mult),
                            waits=[expr_tok, hm_free[hsl]] + ([] if hm_toks else []))
                        hm_toks.append(tap)
                        return tap

                    e1 = p.op("vector", lambda e: e.tensor_add(mw["t0"][:, :], mw["w01"][:, :], mw["w02"][:, :]),
                              waits=[cmp_toks["w01"], cmp_toks["w02"]])
                    e2 = p.op("vector", lambda e: e.tensor_add(mw["t0"][:, :], mw["t0"][:, :], mw["w03"][:, :]),
                              waits=[e1, cmp_toks["w03"]])
                    mfin(e2, 1.5, ALU.is_ge, 0)
                    e3 = p.op("vector", lambda e: e.tensor_add(mw["t0"][:, :], mw["w12"][:, :], mw["w13"][:, :]),
                              waits=[cmp_toks["w12"], cmp_toks["w13"], hm_toks[-1]])
                    e4 = p.op("vector", lambda e: e.tensor_sub(mw["t0"][:, :], mw["t0"][:, :], mw["w01"][:, :]),
                              waits=[e3])
                    mfin(e4, 0.5, ALU.is_ge, 1)
                    e5 = p.op("vector", lambda e: e.tensor_sub(mw["t0"][:, :], mw["w23"][:, :], mw["w02"][:, :]),
                              waits=[cmp_toks["w23"], hm_toks[-1]])
                    e6 = p.op("vector", lambda e: e.tensor_sub(mw["t0"][:, :], mw["t0"][:, :], mw["w12"][:, :]),
                              waits=[e5])
                    mfin(e6, -0.5, ALU.is_ge, 2)
                    e7 = p.op("vector", lambda e: e.tensor_add(mw["t0"][:, :], mw["w03"][:, :], mw["w13"][:, :]),
                              waits=[hm_toks[-1]])
                    e8 = p.op("vector", lambda e: e.tensor_add(mw["t0"][:, :], mw["t0"][:, :], mw["w23"][:, :]),
                              waits=[e7])
                    mfin(e8, 1.5, ALU.is_le, 3)
                    mask_busy = hm_toks[-1]
                    h_free = hm_toks[-1]
                    a_free = hm_toks[-1]
                    tmask = hm_toks[-1]

                    def post5(rt=rt, ig=ig, hsl=hsl, tmask=tmask):
                        nonlocal psT5_free, st5_free
                        tts = None
                        for k in range(8):
                            tts = p.op("tensor", lambda e, k=k, hsl=hsl: e.transpose(
                                psT5[:, k * 128:(k + 1) * 128],
                                hm[hsl][:, k * 128:(k + 1) * 128], ident[:, :]),
                                waits=([tmask, psT5_free] if k == 0 else []), inc=(k == 7))
                        hm_free[hsl] = tts
                        tev = p.op("scalar", lambda e: e.activation(out=st5[:, :], in_=psT5[:, :],
                                                                    func=AF.Copy),
                                   waits=[tts, st5_free])
                        psT5_free = tev
                        tfl = p.op("scalar", lambda e, rt=rt, ig=ig: e.dma_start(
                            out=s_hT[rt, :, ig * 8:(ig + 1) * 8, :],
                            in_=st5[:, :].rearrange("p (ic c) -> p ic c", c=128)), waits=[tev],
                            sem="ldC")
                        st5_free = tfl

                    pend5[0] = post5
                wg_free[wsl] = last_mm
            if pend5[0] is not None:
                pend5[0]()
                pend5[0] = None

        p45_scope.close()
        p.barrier()

        with contextlib.ExitStack() as sc:
            wds = [T(sc, f"wds{i}", [128, NIC * 512], BF16) for i in range(2)]
            hts = [T(sc, f"hts{i}", [128, II], BF16) for i in range(3)]
            x2t = [T(sc, f"x2t{i}", [128, 512], F32) for i in range(2)]
            outb = [T(sc, f"outb{i}", [128, 512], F32) for i in range(2)]
            psY = [T(sc, f"psY{i}", [128, 512], F32, psum=True) for i in range(2)]

            wd_view = wd_d.rearrange("(ic p) o -> p ic o", p=128)

            wd_free = [None, None]
            ht_free = [None, None, None]
            x2t_free = [None, None]
            out_free = [None, None]
            psY_free = [None, None]
            ty_hist = {}

            def ht_load(j):
                oc, rt = divmod(j, NQT)
                hsl = j % 3
                return p.op("sync", lambda e, rt=rt, hsl=hsl: e.dma_start(
                    out=hts[hsl][:, :].rearrange("p (ic c) -> p ic c", c=128),
                    in_=s_hT[rt, :, :, :]), waits=[ht_free[hsl]],
                    sem=("ldA", "ldB", "ldC")[hsl])

            th_tok = {0: ht_load(0), 1: ht_load(1), 2: ht_load(2)}
            for oc in range(4):
                wsl = oc % 2
                osl_cols = slice(oc * 512, (oc + 1) * 512)
                twd = p.op("gpsimd", lambda e, s=osl_cols, wsl=wsl: e.dma_start(
                    out=wds[wsl][:, :].rearrange("p (ic c) -> p ic c", c=512),
                    in_=wd_view[:, :, s]), waits=[wd_free[wsl]], sem=("gwA", "gwB")[wsl])
                last_mm = None
                for rt in range(NQT):
                    j = oc * NQT + rt
                    hsl = j % 3
                    xsl = j % 2
                    tx2 = p.op("sync", lambda e, rt=rt, oc=oc, xsl=xsl: e.dma_start(
                        out=x2t[xsl][:, :],
                        in_=s_x2[rt * 128:(rt + 1) * 128, oc * 512:(oc + 1) * 512]),
                        waits=[x2t_free[xsl]], sem=("ldE", "ldF")[xsl])
                    ty = None
                    psl = j % 2
                    for ic in range(NIC):
                        ty = p.op("tensor", lambda e, ic=ic, wsl=wsl, hsl=hsl, psl=psl: e.matmul(
                            psY[psl][:, :], hts[hsl][:, ic * 128:(ic + 1) * 128],
                            wds[wsl][:, ic * 512:(ic + 1) * 512],
                            start=(ic == 0), stop=(ic == NIC - 1)),
                            waits=([th_tok[j], twd, psY_free[psl]] if ic == 0 else []),
                            inc=(ic == NIC - 1))
                    last_mm = ty
                    ht_free[hsl] = ty
                    if j + 3 < 4 * NQT:
                        th_tok[j + 3] = ht_load(j + 3)
                    tev = p.op("vector", lambda e, xsl=xsl, psl=psl: e.tensor_tensor(
                        outb[xsl][:, :], psY[psl][:, :], x2t[xsl][:, :], ALU.add),
                        waits=[ty, tx2, out_free[xsl]])
                    psY_free[psl] = tev
                    x2t_free[xsl] = tev
                    tso = p.op("scalar", lambda e, rt=rt, oc=oc, xsl=xsl: e.dma_start(
                        out=out_d[rt * 128:(rt + 1) * 128, oc * 512:(oc + 1) * 512],
                        in_=outb[xsl][:, :]), waits=[tev], sem=("ldG", "ldH")[xsl])
                    out_free[xsl] = tso
                wd_free[wsl] = last_mm

        p.barrier()
        p.flush(block, sems)

    return nc


_NC_CACHE = {}


def kernel(L0_attn_x, L0_attn_wq, L0_attn_wk, L0_attn_wv, L0_attn_wo,
           L0_ffn_w_gate, L0_ffn_w_up, L0_ffn_w_down):
    x = np.ascontiguousarray(np.asarray(L0_attn_x, dtype=np.float32))
    wq = np.asarray(L0_attn_wq, dtype=np.float32)
    wk = np.asarray(L0_attn_wk, dtype=np.float32)
    wv = np.asarray(L0_attn_wv, dtype=np.float32)
    wo = np.asarray(L0_attn_wo, dtype=np.float32)
    wg = np.asarray(L0_ffn_w_gate, dtype=np.float32)
    wu = np.asarray(L0_ffn_w_up, dtype=np.float32)
    wd = np.asarray(L0_ffn_w_down, dtype=np.float32)

    hperm = np.concatenate([h * HD + np.concatenate([np.arange(0, HD, 2), np.arange(1, HD, 2)])
                            for h in range(H)])
    wqT = np.ascontiguousarray(wq.T[:, hperm])
    wkT = np.ascontiguousarray(wk.T[:, hperm])
    wvT = np.ascontiguousarray(wv.T)
    woT = np.ascontiguousarray(wo.T)

    iperm = np.empty(II, dtype=np.int64)
    for blk in range(II // 1024):
        base = blk * 1024
        cols = base + np.arange(1024)
        g = cols.reshape(256, 4)
        iperm[base:base + 1024] = np.concatenate([g[:, e] for e in range(4)])
    wg_p = np.ascontiguousarray(wg[:, iperm])
    wu_p = np.ascontiguousarray(wu[:, iperm])
    wd_p = np.ascontiguousarray(wd[iperm, :])

    t64 = np.arange(S, dtype=np.float64)
    inv = 1.0 / (10000.0 ** (np.arange(0, HD, 2, dtype=np.float64) / HD))
    ang = t64[:, None] * inv[None, :]
    cosT = np.cos(ang).astype(np.float32)
    sinT = np.sin(ang).astype(np.float32)
    identity = np.eye(128, dtype=ml_dtypes.bfloat16)

    key = "nc"
    if key not in _NC_CACHE:
        _NC_CACHE[key] = build_nc()
    nc = _NC_CACHE[key]

    in_maps = []
    for c in range(8):
        b, hg = c // 4, c % 4
        xb = np.ascontiguousarray(x[b])
        in_maps.append({
            "x": xb,
            "xq": np.ascontiguousarray(x[b, hg * QR:(hg + 1) * QR, :]),
            "wqT": np.ascontiguousarray(wqT[:, hg * 512:(hg + 1) * 512]),
            "wkT": np.ascontiguousarray(wkT[:, hg * 512:(hg + 1) * 512]),
            "wvT": np.ascontiguousarray(wvT[:, hg * 512:(hg + 1) * 512]),
            "woTs": np.ascontiguousarray(woT[hg * 512:(hg + 1) * 512, :]),
            "wg": wg_p, "wu": wu_p, "wd": wd_p,
            "cosT": cosT, "sinT": sinT,
            "ident": identity,
        })

    res = run_bass_kernel_spmd(nc, in_maps, core_ids=list(range(8)))

    out = np.empty((B, S, D), dtype=np.float32)
    for c in range(8):
        b, j = c // 4, c % 4
        out[b, j * QR:(j + 1) * QR, :] = res.results[c]["out"]
    return out


# version 16
# speedup vs baseline: 1.0460x; 1.0460x over previous
import contextlib
import numpy as np
import ml_dtypes
import concourse.bass as bass
import concourse.mybir as mybir
from concourse.bass_utils import run_bass_kernel_spmd

F32 = mybir.dt.float32
BF16 = mybir.dt.bfloat16
AF = mybir.ActivationFunctionType
ALU = mybir.AluOpType
AX = mybir.AxisListType

B, S, D, H, HD, II = 2, 4096, 2048, 16, 128, 8192
QR = S // 4
NT = S // 128
NQT = QR // 128
NHC = D // 128
NIC = II // 128
EPS = 1e-5
ESC = float(1.0 / np.sqrt(HD))

ENGS = ("sync", "gpsimd", "scalar", "vector", "tensor")
ENG_SEM = {"sync": "ldA", "gpsimd": "gwA", "scalar": "ss", "vector": "vs", "tensor": "ts"}
DMA_SEMS = ("ldZ", "ldA", "ldB", "ldC", "ldD", "ldE", "ldF", "ldG", "ldH",
            "ldI", "ldJ", "ldK", "ldL", "ldM",
            "gwA", "gwB", "gwC", "gwD", "gwE", "gwF")
SEMS = DMA_SEMS + ("ss", "vs", "ts", "gc")


def _flat(waits):
    out = []
    for t in waits:
        if t is None:
            continue
        if isinstance(t, (list, tuple)) and t and isinstance(t[0], (list, tuple)):
            out.extend(x for x in t if x is not None)
        elif isinstance(t, tuple) and len(t) == 2 and isinstance(t[0], str):
            out.append(t)
        elif isinstance(t, (list, tuple)):
            out.extend(x for x in t if x is not None)
    return out


class Prog:

    def __init__(self):
        self.q = {e: [] for e in ENGS}
        self.cnt = {s: 0 for s in SEMS}

    def op(self, eng, fn, waits=(), inc=True, sem=None):
        if sem is None:
            sem = ENG_SEM[eng]
        amt = 16 if sem in DMA_SEMS else 1
        w = _flat(waits)
        if inc:
            self.cnt[sem] += amt
            tok = (sem, self.cnt[sem])
        else:
            tok = None
        self.q[eng].append(("op", fn, w, sem if inc else None, amt))
        return tok

    def barrier(self, exclude=()):
        snap = {s: v for s, v in self.cnt.items() if s not in exclude}
        for e in ENGS:
            self.q[e].append(("bar", snap))

    def flush(self, block, sems):
        for e in ENGS:
            items = self.q[e]
            if not items:
                continue

            def emit(eng, items=items):
                waited = {}
                for it in items:
                    if it[0] == "bar":
                        for s, v in it[1].items():
                            if v > waited.get(s, 0):
                                eng.wait_ge(sems[s], v)
                                waited[s] = v
                    else:
                        _, fn, w, sem, amt = it
                        for (s, v) in w:
                            if v > waited.get(s, 0):
                                eng.wait_ge(sems[s], v)
                                waited[s] = v
                        inst = fn(eng)
                        if sem is not None:
                            inst.then_inc(sems[sem], amt)

            getattr(block, e)(emit)


def build_nc():
    nc = bass.Bass("TRN2", target_bir_lowering=False, debug=False)
    p = Prog()

    def T(sc, name, shape, dtype, psum=False):
        f = nc.psum_tensor if psum else nc.sbuf_tensor
        return sc.enter_context(f(name, shape, dtype))

    x_d = nc.dram_tensor("x", [S, D], F32, kind="ExternalInput").ap()
    xq_d = nc.dram_tensor("xq", [QR, D], F32, kind="ExternalInput").ap()
    wqT_d = nc.dram_tensor("wqT", [D, 512], F32, kind="ExternalInput").ap()
    wkT_d = nc.dram_tensor("wkT", [D, 512], F32, kind="ExternalInput").ap()
    wvT_d = nc.dram_tensor("wvT", [D, 512], F32, kind="ExternalInput").ap()
    woTs_d = nc.dram_tensor("woTs", [512, D], F32, kind="ExternalInput").ap()
    wg_d = nc.dram_tensor("wg", [D, II], F32, kind="ExternalInput").ap()
    wu_d = nc.dram_tensor("wu", [D, II], F32, kind="ExternalInput").ap()
    wd_d = nc.dram_tensor("wd", [II, D], F32, kind="ExternalInput").ap()
    cos_d = nc.dram_tensor("cosT", [S, 64], F32, kind="ExternalInput").ap()
    sin_d = nc.dram_tensor("sinT", [S, 64], F32, kind="ExternalInput").ap()
    id_d = nc.dram_tensor("ident", [128, 128], BF16, kind="ExternalInput").ap()
    out_d = nc.dram_tensor("out", [QR, D], F32, kind="ExternalOutput").ap()

    s_ln1T = nc.dram_tensor("s_ln1T", [NT, 128, D], BF16, kind="Internal").ap()
    s_kT = nc.dram_tensor("s_kT", [4, 128, S], BF16, kind="Internal").ap()
    s_qT = nc.dram_tensor("s_qT", [4, 128, S], BF16, kind="Internal").ap()
    s_v = nc.dram_tensor("s_v", [2, 128, NT, 256], BF16, kind="Internal").ap()
    rin = [nc.dram_tensor(f"rin{h}", [2048, D], BF16, kind="Internal").ap() for h in range(2)]
    rout = [nc.dram_tensor(f"rout{h}", [512, D], BF16, kind="Internal").ap() for h in range(2)]
    s_x2 = nc.dram_tensor("s_x2", [QR, D], F32, kind="Internal").ap()
    s_ln2T = nc.dram_tensor("s_ln2T", [NQT, 128, D], BF16, kind="Internal").ap()
    s_hT = nc.dram_tensor("s_hT", [NQT, 128, NIC, 128], BF16, kind="Internal").ap()

    with contextlib.ExitStack() as stack:
        sems = {s: stack.enter_context(nc.semaphore(s)) for s in SEMS}

        cpool = stack.enter_context(nc.sbuf_tensor("cpool", [128, 4], F32))
        ident = stack.enter_context(nc.sbuf_tensor("idnt", [128, 128], BF16))
        ones = stack.enter_context(nc.sbuf_tensor("ones", [128, 128], BF16))
        stat = stack.enter_context(nc.sbuf_tensor("stat", [128, 32], F32))

        nc.const_aps.register(F32, float(EPS), cpool[:, 0:1])
        nc.const_aps.register(F32, float(ESC), cpool[:, 1:2])

        block = stack.enter_context(nc.Block())

        t_id = p.op("sync", lambda e: e.dma_start(out=ident[:, :], in_=id_d[:, :]), sem="ldZ")
        p.op("vector", lambda e: e.memset(cpool[:, 0:1], float(EPS)))
        p.op("vector", lambda e: e.memset(cpool[:, 1:2], float(ESC)))
        t_ones = p.op("vector", lambda e: e.memset(ones[:, :], 1.0))

        def ln_tile(xsrc, sqd, dst_bf16, waits, so=0):
            c = lambda k: stat[:, so + k:so + k + 1]
            t1 = p.op("vector", lambda e: e.reduce_sum(c(0), xsrc[:, :], axis=AX.X),
                      waits=waits)
            t2 = p.op("vector", lambda e: e.tensor_scalar_mul(c(0), c(0), 1.0 / D),
                      waits=[t1])
            t3 = p.op("scalar", lambda e: e.activation(out=sqd[:, :], in_=xsrc[:, :], func=AF.Square,
                                                       accum_out=c(1)), waits=waits + [t1])
            t4 = p.op("vector", lambda e: e.tensor_scalar_mul(c(1), c(1), 1.0 / D),
                      waits=[t3])
            t5 = p.op("vector", lambda e: e.tensor_tensor(c(2), c(0), c(0), ALU.mult),
                      waits=[t2])
            t6 = p.op("vector", lambda e: e.tensor_sub(c(2), c(1), c(2)),
                      waits=[t4, t5])
            t7 = p.op("scalar", lambda e: e.activation(out=c(3), in_=c(2), func=AF.Sqrt,
                                                       bias=float(EPS)), waits=[t6])
            t8 = p.op("vector", lambda e: e.reciprocal(c(4), c(3)), waits=[t7])
            t9 = p.op("vector", lambda e: e.tensor_tensor(c(5), c(0), c(4), ALU.mult),
                      waits=[t8])
            t10 = p.op("vector", lambda e: e.tensor_scalar_mul(c(5), c(5), -1.0),
                       waits=[t9])
            ta = p.op("scalar", lambda e: e.activation(out=dst_bf16[:, :], in_=xsrc[:, :], func=AF.Identity,
                                                       bias=c(5), scale=c(4)),
                      waits=[t10])
            return ta

        p12_scope = contextlib.ExitStack()
        cosF = p12_scope.enter_context(nc.sbuf_tensor("cosF", [128, NT * 64], F32))
        sinF = p12_scope.enter_context(nc.sbuf_tensor("sinF", [128, NT * 64], F32))
        wk0_sb = p12_scope.enter_context(nc.sbuf_tensor("wk0_sb", [128, NHC * 512], BF16))
        wv0_sb = p12_scope.enter_context(nc.sbuf_tensor("wv0_sb", [128, NHC * 512], BF16))
        wq0_sb = p12_scope.enter_context(nc.sbuf_tensor("wq0_sb", [128, NHC * 512], BF16))
        wq_view = wqT_d.rearrange("(hc p) d -> p hc d", p=128)
        wk_view = wkT_d.rearrange("(hc p) d -> p hc d", p=128)
        wv_view = wvT_d.rearrange("(hc p) d -> p hc d", p=128)

        def wsbv(t):
            return t[:, :].rearrange("p (hc c) -> p hc c", c=512)

        tbl_toks = [
            p.op("gpsimd", lambda e: e.dma_start(
                out=cosF[:, :].rearrange("p (t z) -> p t z", z=64),
                in_=cos_d.rearrange("(t p) z -> p t z", p=128)), sem="gwD"),
            p.op("gpsimd", lambda e: e.dma_start(
                out=sinF[:, :].rearrange("p (t z) -> p t z", z=64),
                in_=sin_d.rearrange("(t p) z -> p t z", p=128)), sem="gwD"),
        ]
        t_tbl = tbl_toks[-1]
        t_wk0 = p.op("gpsimd", lambda e: e.dma_start(
            out=wsbv(wk0_sb), in_=wk_view[:, :, 0:512]), waits=[], sem="gwA")
        t_wv0 = p.op("gpsimd", lambda e: e.dma_start(
            out=wsbv(wv0_sb), in_=wv_view[:, :, 0:512]), waits=[], sem="gwB")
        t_wq0 = p.op("gpsimd", lambda e: e.dma_start(
            out=wsbv(wq0_sb), in_=wq_view[:, :, 0:512]), waits=[], sem="gwC")
        with contextlib.ExitStack() as sc:
            NP1 = 6
            xin = [T(sc, f"xin{i}", [128, D], F32) for i in range(NP1)]
            sqd = [T(sc, f"sqd{i}", [128, D], F32) for i in range(2)]
            ln_sb = [T(sc, f"lnsb{i}", [128, D], BF16) for i in range(NP1)]
            lnT_sb = [T(sc, f"lnT{i}", [128, D], BF16) for i in range(NP1)]
            psTl = [T(sc, f"psTl{i}", [128, 1024], BF16, psum=True) for i in range(2)]

            xin_free = [None] * NP1
            lnapply_war = [None] * NP1
            lnT_free = [None] * NP1
            prev_ln = [None] * NP1
            prev_stat = [None] * 4
            sq_war = [None, None]
            psTl_free = [None, None]

            def p1_a(i, src_rows):
                sl = i % NP1
                tl = p.op("sync", lambda e, sl=sl: e.dma_start(out=xin[sl][:, :], in_=src_rows),
                          waits=[xin_free[sl]], sem=("ldA", "ldB", "ldC", "ldD", "ldE", "ldF")[sl])
                return {"sl": sl, "so": 8 * (i % 4), "tl": tl}

            def p1_b(st):
                sl, so = st["sl"], st["so"]
                c = lambda k: stat[:, so + k:so + k + 1]
                st["c"] = c
                w0 = [st["tl"], prev_ln[sl], prev_stat[(st["so"] // 8) % 4], lnapply_war[sl]]
                st["t1"] = p.op("vector", lambda e, sl=sl, c=c: e.reduce_sum(
                    c(0), xin[sl][:, :], axis=AX.X), waits=w0)
                st["t3"] = p.op("scalar", lambda e, sl=sl, c=c: e.activation(
                    out=sqd[sl % 2][:, :], in_=xin[sl][:, :], func=AF.Square, accum_out=c(1)),
                    waits=w0 + [sq_war[sl % 2]])

            def p1_c(st):
                sl, c = st["sl"], st["c"]
                t2 = p.op("vector", lambda e, c=c: e.tensor_scalar_mul(c(0), c(0), 1.0 / D),
                          waits=[st["t1"]])
                t4 = p.op("vector", lambda e, c=c: e.tensor_scalar_mul(c(1), c(1), 1.0 / D),
                          waits=[st["t3"]])
                t5 = p.op("vector", lambda e, c=c: e.tensor_tensor(c(2), c(0), c(0), ALU.mult),
                          waits=[t2])
                t6 = p.op("vector", lambda e, c=c: e.tensor_sub(c(2), c(1), c(2)),
                          waits=[t4, t5])
                t7 = p.op("scalar", lambda e, c=c: e.activation(out=c(3), in_=c(2), func=AF.Sqrt,
                                                                bias=float(EPS)), waits=[t6])
                t8 = p.op("vector", lambda e, c=c: e.reciprocal(c(4), c(3)), waits=[t7])
                t9 = p.op("vector", lambda e, c=c: e.tensor_tensor(c(5), c(0), c(4), ALU.mult),
                          waits=[t8])
                t10 = p.op("vector", lambda e, c=c: e.tensor_scalar_mul(c(5), c(5), -1.0),
                           waits=[t9])
                ta = p.op("scalar", lambda e, sl=sl, c=c: e.activation(
                    out=ln_sb[sl][:, :], in_=xin[sl][:, :], func=AF.Identity,
                    bias=c(5), scale=c(4)), waits=[t10])
                prev_ln[sl] = ta
                prev_stat[(st["so"] // 8) % 4] = ta
                xin_free[sl] = ta
                sq_war[sl % 2] = ta
                st["ta"] = ta

            def p1_d(st, dst_block):
                sl, ta = st["sl"], st["ta"]
                pl = sl % 2
                ev_toks = []
                last_tt = None
                for g in range(2):
                    tts = None
                    for k in range(8):
                        hc = g * 8 + k
                        tts = p.op("tensor", lambda e, hc=hc, k=k, sl=sl, pl=pl: e.transpose(
                            psTl[pl][:, k * 128:(k + 1) * 128],
                            ln_sb[sl][:, hc * 128:(hc + 1) * 128], ident[:, :]),
                            waits=([ta, t_id, psTl_free[pl]] if k == 0 else []),
                            inc=(k == 7))
                    last_tt = tts
                    tev = p.op("vector", lambda e, g=g, sl=sl, pl=pl: e.tensor_copy(
                        lnT_sb[sl][:, g * 1024:(g + 1) * 1024], psTl[pl][:, :]),
                        waits=[tts, lnT_free[sl]])
                    psTl_free[pl] = tev
                    ev_toks.append(tev)
                lnapply_war[sl] = last_tt
                tw = p.op("scalar", lambda e, sl=sl: e.dma_start(out=dst_block, in_=lnT_sb[sl][:, :]),
                          waits=ev_toks, sem=("ldI", "ldJ", "ldK", "ldL", "ldG", "ldH")[sl])
                lnT_free[sl] = tw
                return tw

            srcs = [(x_d[t * 128:(t + 1) * 128, :], s_ln1T[t, :, :]) for t in range(NT)]
            states = {}
            NTOT = len(srcs)
            for i in range(NTOT + 4):
                if i < NTOT:
                    states[i] = p1_a(i, srcs[i][0])
                if 0 <= i - 2 < NTOT:
                    p1_b(states[i - 2])
                if 0 <= i - 3 < NTOT:
                    p1_c(states[i - 3])
                if 0 <= i - 4 < NTOT:
                    p1_d(states[i - 4], srcs[i - 4][1])
                    del states[i - 4]

        p.barrier()

        with contextlib.ExitStack() as sc:
            lnP2 = [T(sc, f"p2lnT{i}", [128, D], BF16) for i in range(2)]
            rA = T(sc, "rA", [128, 256], F32)
            rB = T(sc, "rB", [128, 256], F32)
            rC = T(sc, "rC", [128, 256], F32)
            rD = T(sc, "rD", [128, 256], F32)
            K_sb = [T(sc, f"K_sb{i}", [128, 512], BF16) for i in range(2)]
            Q_sb = [T(sc, f"Q_sb{i}", [128, 512], BF16) for i in range(2)]
            V_sb = [T(sc, f"V_sb{i}", [128, 512], BF16) for i in range(2)]
            kT_st = T(sc, "kT_st", [128, 2048], BF16)
            qT_st = T(sc, "qT_st", [128, 2048], BF16)
            psK = [T(sc, f"psK{i}", [128, 512], F32, psum=True) for i in range(2)]
            psV = [T(sc, f"psV{i}", [128, 512], F32, psum=True) for i in range(2)]
            psQ = [T(sc, f"psQ{i}", [128, 512], F32, psum=True) for i in range(2)]
            psTk = T(sc, "psTk", [128, 1024], BF16, psum=True)

            lnP2_free = [None, None]
            psK_free = [None, None]
            psV_free = [None, None]
            psQ_free = [None, None]
            rope_busy = None
            k_free = [None, None]
            q_free = [None, None]
            v_free = [None, None]
            kst_free = None
            qst_free = None
            psTk_free = None
            pend = [None]

            def rope_emit(ps, idx, out_sb, waits):
                nonlocal rope_busy
                pr = ps[:, :].rearrange("p (h two z) -> p h two z", two=2, z=64)
                re_, im_ = pr[:, :, 0, :], pr[:, :, 1, :]
                ct_ap = cosF[:, :].rearrange("p (t z) -> p t z", z=64)[:, idx, :]
                st_ap = sinF[:, :].rearrange("p (t z) -> p t z", z=64)[:, idx, :]
                cb = ct_ap.unsqueeze(1).broadcast_to([128, 4, 64])
                sb = st_ap.unsqueeze(1).broadcast_to([128, 4, 64])
                rAv = rA[:, :].rearrange("p (h z) -> p h z", z=64)
                rBv = rB[:, :].rearrange("p (h z) -> p h z", z=64)
                rCv = rC[:, :].rearrange("p (h z) -> p h z", z=64)
                rDv = rD[:, :].rearrange("p (h z) -> p h z", z=64)
                ov = out_sb[:, :].rearrange("p (h two z) -> p h two z", two=2, z=64)
                w0 = list(waits) + [rope_busy, t_tbl]
                t1 = p.op("vector", lambda e: e.tensor_tensor(rAv, re_, cb, ALU.mult), waits=w0)
                t2 = p.op("vector", lambda e: e.tensor_tensor(rBv, im_, sb, ALU.mult), waits=[])
                t3 = p.op("vector", lambda e: e.tensor_tensor(rCv, re_, sb, ALU.mult), waits=[])
                t4 = p.op("vector", lambda e: e.tensor_tensor(rDv, im_, cb, ALU.mult), waits=[])
                t5 = p.op("vector", lambda e: e.tensor_tensor(ov[:, :, 0, :], rAv, rBv, ALU.subtract),
                          waits=[t1, t2])
                t6 = p.op("vector", lambda e: e.tensor_tensor(ov[:, :, 1, :], rCv, rDv, ALU.add),
                          waits=[t3, t4])
                rope_busy = t6
                return t6

            def mm_group(ps, wt, t_w, sl, tln, free_tok):
                tk = None
                for hc in range(NHC):
                    tk = p.op("tensor", lambda e, hc=hc, sl=sl, ps=ps, wt=wt: e.matmul(
                        ps[sl][:, :], lnP2[sl][:, hc * 128:(hc + 1) * 128],
                        wt[:, hc * 512:(hc + 1) * 512],
                        start=(hc == 0), stop=(hc == NHC - 1)),
                        waits=([tln, t_w, free_tok] if hc == 0 else []),
                        inc=(hc == NHC - 1))
                return tk

            for kt in range(NT):
                sl = kt % 2
                tln = p.op("sync", lambda e, kt=kt, sl=sl: e.dma_start(
                    out=lnP2[sl][:, :], in_=s_ln1T[kt, :, :]), waits=[lnP2_free[sl]],
                    sem=("ldA", "ldB")[sl])
                tk = mm_group(psK, wk0_sb, t_wk0, sl, tln, psK_free[sl])
                tv = mm_group(psV, wv0_sb, t_wv0, sl, tln, psV_free[sl])
                tq = mm_group(psQ, wq0_sb, t_wq0, sl, tln, psQ_free[sl])
                lnP2_free[sl] = tq
                trk = rope_emit(psK[sl], kt, K_sb[sl], [tk, k_free[sl]])
                psK_free[sl] = trk
                tve = p.op("vector", lambda e, sl=sl: e.tensor_copy(V_sb[sl][:, :], psV[sl][:, :]),
                           waits=[tv, v_free[sl]])
                psV_free[sl] = tve
                vw = []
                for vp in range(2):
                    vw.append(p.op("scalar", lambda e, vp=vp, sl=sl, kt=kt: e.dma_start(
                        out=s_v[vp, :, kt, :],
                        in_=V_sb[sl][:, vp * 256:(vp + 1) * 256]), waits=[tve],
                        sem=("ldG", "ldH")[sl]))
                v_free[sl] = vw[-1]
                trq = rope_emit(psQ[sl], kt, Q_sb[sl], [tq, q_free[sl]])
                psQ_free[sl] = trq
                if pend[0] is not None:
                    pend[0]()
                    pend[0] = None

                def post(kt=kt, sl=sl, trk=trk, trq=trq):
                    nonlocal psTk_free, kst_free, qst_free
                    w_i = kt % 4
                    tts = None
                    for hs in range(4):
                        tts = p.op("tensor", lambda e, hs=hs, sl=sl: e.transpose(
                            psTk[:, hs * 128:(hs + 1) * 128],
                            K_sb[sl][:, hs * 128:(hs + 1) * 128], ident[:, :]),
                            waits=([trk, psTk_free] if hs == 0 else []), inc=False)
                    for hs in range(4):
                        tts = p.op("tensor", lambda e, hs=hs, sl=sl: e.transpose(
                            psTk[:, 512 + hs * 128:512 + (hs + 1) * 128],
                            Q_sb[sl][:, hs * 128:(hs + 1) * 128], ident[:, :]),
                            waits=([trq] if hs == 0 else []), inc=(hs == 3))
                    k_free[sl] = tts
                    q_free[sl] = tts
                    tev = p.op("vector", lambda e, w_i=w_i: e.tensor_copy(
                        kT_st[:, :].rearrange("p (hs w c) -> p hs w c", w=4, c=128)[:, :, w_i, :],
                        psTk[:, 0:512].rearrange("p (hs c) -> p hs c", c=128)),
                        waits=[tts, kst_free])
                    tev2 = p.op("vector", lambda e, w_i=w_i: e.tensor_copy(
                        qT_st[:, :].rearrange("p (hs w c) -> p hs w c", w=4, c=128)[:, :, w_i, :],
                        psTk[:, 512:1024].rearrange("p (hs c) -> p hs c", c=128)),
                        waits=[qst_free])
                    psTk_free = tev2
                    if w_i == 3:
                        fl = []
                        for hs in range(4):
                            fl.append(p.op("scalar", lambda e, hs=hs, kt=kt: e.dma_start(
                                out=s_kT[hs, :, (kt - 3) * 128:(kt + 1) * 128],
                                in_=kT_st[:, hs * 512:(hs + 1) * 512]), waits=[tev],
                                sem="ldC"))
                        kst_free = fl[-1]
                        fl2 = []
                        for hs in range(4):
                            fl2.append(p.op("scalar", lambda e, hs=hs, kt=kt: e.dma_start(
                                out=s_qT[hs, :, (kt - 3) * 128:(kt + 1) * 128],
                                in_=qT_st[:, hs * 512:(hs + 1) * 512]), waits=[tev2],
                                sem="ldD"))
                        qst_free = fl2[-1]

                pend[0] = post
            if pend[0] is not None:
                pend[0]()
                pend[0] = None

        p12_scope.close()
        p.barrier()


        p45_scope = contextlib.ExitStack()
        wgs0 = p45_scope.enter_context(nc.sbuf_tensor("wgs0", [128, NHC * 1024], BF16))
        wus0_hold = [p45_scope.enter_context(nc.sbuf_tensor("wus0", [128, NHC * 1024], BF16))]
        wg_view = wg_d.rearrange("(hc p) i -> p hc i", p=128)
        wu_view = wu_d.rearrange("(hc p) i -> p hc i", p=128)
        twg0_holder = [None, None]
        RG = [[0, 1, 2, 3], [4, 5, 6, 7]]
        o_scope = contextlib.ExitStack()
        woTs_sb = o_scope.enter_context(nc.sbuf_tensor("woTs_sb", [128, 4 * D], BF16))
        t_wo = p.op("gpsimd", lambda e: e.dma_start(
            out=woTs_sb[:, :].rearrange("p (c o) -> p c o", o=D),
            in_=woTs_d.rearrange("(c p) o -> p c o", p=128)), waits=[], sem="gwD")
        rs_tok = [None, None]
        with contextlib.ExitStack() as sc:
            KA = T(sc, "KA", [128, 4 * S], BF16)
            QA = T(sc, "QA", [128, 4 * S], BF16)
            VA = T(sc, "VA", [128, NT * 512], BF16)
            O_st = [T(sc, f"Ost{i}", [128, 4 * 512], BF16) for i in range(2)]
            ap_sb = [T(sc, f"apsb{i}", [128, D], BF16) for i in range(2)]
            expT = [T(sc, f"expT{i}", [128, 1024], BF16) for i in range(2)]
            rs = T(sc, "rs", [128, 512], F32)
            psS = [T(sc, f"psS{i}", [128, 1024], F32, psum=True) for i in range(2)]
            psSum = [T(sc, f"psSum{i}", [128, 512], F32, psum=True) for i in range(2)]
            psO = [T(sc, f"psO{i}", [128, 512], F32, psum=True) for i in range(2)]

            tK = [p.op("sync", lambda e, h=h: e.dma_start(
                out=KA[:, h * S:(h + 1) * S], in_=s_kT[h, :, :]),
                sem=("ldA", "ldB", "ldC", "ldD")[h]) for h in range(4)]
            tQ = [p.op("sync", lambda e, h=h: e.dma_start(
                out=QA[:, h * S:(h + 1) * S], in_=s_qT[h, :, :]),
                sem=("ldE", "ldF", "ldG", "ldH")[h]) for h in range(4)]
            tV = [p.op("sync", lambda e, vp=vp: e.dma_start(
                out=VA[:, :].rearrange("p (kc g c) -> p kc g c", g=2, c=256)[:, :, vp, :],
                in_=s_v[vp, :, :, :]), sem=("ldI", "ldJ")[vp]) for vp in range(2)]

            psS_free = [None, None]
            expT_free = [None] * 2
            psSum_free = [None, None]
            psO_free = [None, None]
            rs_free = None
            O_free = [[], []]
            ap_free = [[], []]
            store_half = {0: [], 1: []}
            qidx = 0

            qt_order = [0, 2, 4, 6, 1, 3, 5, 7]
            for n_q, qt in enumerate(qt_order):
                osl_q = n_q % 2
                o_done = []
                for h in range(4):
                    osl = qidx % 2
                    qidx += 1
                    tload = [tK[h], tQ[h], tV[h // 2]]

                    def s_mm(kp):
                        tS = None
                        for half in range(2):
                            kc = 2 * kp + half
                            ss_ = kp % 2
                            tS = p.op("tensor", lambda e, kc=kc, half=half, ss_=ss_, h=h, qt=qt: e.matmul(
                                psS[ss_][:, half * 512:(half + 1) * 512],
                                KA[:, h * S + kc * 128:h * S + (kc + 1) * 128],
                                QA[:, h * S + qt * 512:h * S + (qt + 1) * 512],
                                start=True, stop=True),
                                waits=(tload + [psS_free[ss_]] if half == 0 else []),
                                inc=(half == 1))
                        return tS

                    tS_pend = {0: s_mm(0), 1: s_mm(1)}
                    tsum = tpv = None
                    for kp in range(16):
                        es = kp % 2
                        ss_ = kp % 2
                        texp = p.op("scalar", lambda e, ss_=ss_, es=es: e.activation(
                            out=expT[es][:, :], in_=psS[ss_][:, :], func=AF.Exp,
                            scale=float(ESC)), waits=[tS_pend.pop(kp), expT_free[es]])
                        psS_free[ss_] = texp
                        for half in range(2):
                            tsum = p.op("tensor", lambda e, es=es, half=half, kp=kp, osl=osl: e.matmul(
                                psSum[osl][:, :], ones[:, :], expT[es][:, half * 512:(half + 1) * 512],
                                start=(kp == 0 and half == 0), stop=(kp == 15 and half == 1)),
                                waits=(([texp, t_ones] + ([psSum_free[osl]] if kp == 0 else []))
                                       if half == 0 else []),
                                inc=(kp == 15 and half == 1))
                        for half in range(2):
                            kc = 2 * kp + half
                            tpv = p.op("tensor", lambda e, kc=kc, es=es, half=half, h=h, kp=kp, osl=osl: e.matmul(
                                psO[osl][:, :],
                                VA[:, kc * 512 + (h // 2) * 256 + (h % 2) * 128:
                                    kc * 512 + (h // 2) * 256 + (h % 2) * 128 + 128],
                                expT[es][:, half * 512:(half + 1) * 512],
                                start=(kp == 0 and half == 0), stop=(kp == 15 and half == 1)),
                                waits=(([texp] + ([psO_free[osl]] if kp == 0 else []))
                                       if half == 0 else []),
                                inc=(kp == 15 and half == 1))
                        expT_free[es] = tpv
                        if kp + 2 < 16:
                            tS_pend[kp + 2] = s_mm(kp + 2)
                    trc = p.op("vector", lambda e, osl=osl: e.reciprocal(rs[:, :], psSum[osl][:, :]),
                               waits=[tsum, rs_free])
                    psSum_free[osl] = trc
                    tno = p.op("vector", lambda e, h=h, osl=osl, osl_q=osl_q: e.tensor_tensor(
                        O_st[osl_q][:, h * 512:(h + 1) * 512],
                        psO[osl][:, :], rs[:, :], ALU.mult),
                        waits=[tpv, trc] + (O_free[osl_q] if h == 0 else []))
                    psO_free[osl] = tno
                    rs_free = tno
                    o_done.append(tno)
                half = qt % 2
                j = qt // 2
                p35_last = []
                for st_i in range(4):
                    asl = (n_q * 4 + st_i) % 2
                    ev_toks = []
                    for oc in range(4):
                        gslot = (st_i * 4 + oc) % 4
                        ps_ap = psS[gslot // 2][:, (gslot % 2) * 512:(gslot % 2 + 1) * 512]
                        tw = None
                        for hl in range(4):
                            tw = p.op("tensor", lambda e, hl=hl, st_i=st_i, ps_ap=ps_ap, osl_q=osl_q, oc=oc: e.matmul(
                                ps_ap,
                                O_st[osl_q][:, hl * 512 + st_i * 128:hl * 512 + (st_i + 1) * 128],
                                woTs_sb[:, hl * D + oc * 512:hl * D + (oc + 1) * 512],
                                start=(hl == 0), stop=(hl == 3)),
                                waits=([t_wo, psS_free[gslot // 2], o_done[-1]]
                                       + (ap_free[asl] if oc == 0 else []) if hl == 0 else []),
                                inc=(hl == 3))
                        tev = p.op("vector", lambda e, oc=oc, asl=asl, ps_ap=ps_ap: e.tensor_copy(
                            ap_sb[asl][:, oc * 512:(oc + 1) * 512], ps_ap),
                            waits=[tw])
                        psS_free[gslot // 2] = tev
                        ev_toks.append(tev)
                    row0 = j * 512 + st_i * 128
                    tst = p.op("scalar", lambda e, asl=asl, half=half, row0=row0: e.dma_start(
                        out=rin[half][row0:row0 + 128, :], in_=ap_sb[asl][:, :]),
                        waits=ev_toks, sem=("ldK", "ldL")[asl])
                    ap_free[asl] = [tst]
                    store_half[half].append(tst)
                    p35_last.append(tst)
                O_free[osl_q] = [p35_last[-1]]
                if n_q == 0:
                    twg0_holder[0] = p.op("gpsimd", lambda e: e.dma_start(
                        out=wgs0[:, :].rearrange("p (hc c) -> p hc c", c=1024),
                        in_=wg_view[:, :, 0:1024]), waits=[p35_last[-1]], sem="gwB")
                    twg0_holder[1] = p.op("gpsimd", lambda e: e.dma_start(
                        out=wus0_hold[0][:, :].rearrange("p (hc c) -> p hc c", c=1024),
                        in_=wu_view[:, :, 0:1024]), waits=[], sem="gwC")
                if n_q == 3:
                    rs_tok[0] = p.op("gpsimd", lambda e: e.collective_compute(
                        "ReduceScatter", ALU.add, replica_groups=RG,
                        ins=[rin[0][:, :]], outs=[rout[0][:, :]]),
                        waits=list(store_half[0]), sem="gc")
            rs2_waits = list(store_half[1])

        o_scope.close()
        p.barrier(exclude=("gc",))

        p.barrier(exclude=("gc",))

        with contextlib.ExitStack() as sc:
            wgs = [wgs0, T(sc, "wgs1", [128, NHC * 1024], BF16)]
            wus = [wus0_hold[0], T(sc, "wus1", [128, NHC * 1024], BF16)]
            tw_ig1 = (
                p.op("gpsimd", lambda e: e.dma_start(
                    out=wgs[1][:, :].rearrange("p (hc c) -> p hc c", c=1024),
                    in_=wg_view[:, :, 1024:2048]), waits=[], sem="gwA"),
                p.op("gpsimd", lambda e: e.dma_start(
                    out=wus[1][:, :].rearrange("p (hc c) -> p hc c", c=1024),
                    in_=wu_view[:, :, 1024:2048]), waits=[], sem="gwD"),
            )
            rs_tok[1] = p.op("gpsimd", lambda e: e.collective_compute(
                "ReduceScatter", ALU.add, replica_groups=RG,
                ins=[rin[1][:, :]], outs=[rout[1][:, :]]),
                waits=rs2_waits, sem="gc")
            at_t = [T(sc, f"att{i}", [128, D], BF16) for i in range(2)]
            x_t = [T(sc, f"xt{i}", [128, D], F32) for i in range(2)]
            x2_sb = T(sc, "x2sb", [128, D], F32)
            ln2_sb = [T(sc, f"ln2sb{i}", [128, D], BF16) for i in range(2)]
            ln2T_sb = T(sc, "ln2T_sb", [128, D], BF16)
            psT2 = T(sc, "psT2", [128, 1024], BF16, psum=True)

            at_free = [None, None]
            xt_free = [None, None]
            x2_free = []
            ln2_war = [None, None]
            ln2T_free = None
            psT2_free = None
            ln2war2 = None
            pend4 = [None]
            tl2_tok = {}

            def p4_tile(qt):
                sl = qt % 2
                ta_ld = p.op("sync", lambda e, qt=qt, sl=sl: e.dma_start(
                    out=at_t[sl][:, :],
                    in_=rout[qt // 4][(qt % 4) * 128:(qt % 4 + 1) * 128, :]),
                    waits=[at_free[sl], rs_tok[qt // 4]], sem=("ldE", "ldF")[sl])
                tx = p.op("sync", lambda e, qt=qt, sl=sl: e.dma_start(
                    out=x_t[sl][:, :], in_=xq_d[qt * 128:(qt + 1) * 128, :]), waits=[xt_free[sl]],
                    sem=("ldG", "ldH")[sl])
                tev = p.op("vector", lambda e, sl=sl: e.tensor_tensor(
                    x2_sb[:, :], x_t[sl][:, :], at_t[sl][:, :], ALU.add),
                    waits=[ta_ld, tx] + x2_free)
                at_free[sl] = tev
                tst = p.op("scalar", lambda e, qt=qt: e.dma_start(
                    out=s_x2[qt * 128:(qt + 1) * 128, :], in_=x2_sb[:, :]), waits=[tev],
                    sem=("ldI", "ldJ")[sl])
                ta = ln_tile(x2_sb, x_t[sl], ln2_sb[sl], [tev, ln2_war[sl], ln2war2],
                             so=8 * sl)
                ln2_war[sl] = ta
                xt_free[sl] = ta
                x2_free[:] = [tst, ta]
                if pend4[0] is not None:
                    pend4[0]()
                    pend4[0] = None

                def post4(qt=qt, sl=sl, ta=ta):
                    nonlocal ln2war2, psT2_free, ln2T_free
                    ev2 = []
                    last_tt = None
                    for g in range(2):
                        tts = None
                        for k in range(8):
                            hc = g * 8 + k
                            tts = p.op("tensor", lambda e, hc=hc, k=k, sl=sl: e.transpose(
                                psT2[:, k * 128:(k + 1) * 128],
                                ln2_sb[sl][:, hc * 128:(hc + 1) * 128], ident[:, :]),
                                waits=([ta, psT2_free] if k == 0 else []), inc=(k == 7))
                        last_tt = tts
                        tev2 = p.op("vector", lambda e, g=g: e.tensor_copy(
                            ln2T_sb[:, g * 1024:(g + 1) * 1024], psT2[:, :]),
                            waits=[tts, ln2T_free])
                        psT2_free = tev2
                        ev2.append(tev2)
                    ln2war2 = last_tt
                    tl2 = p.op("scalar", lambda e, qt=qt: e.dma_start(
                        out=s_ln2T[qt, :, :], in_=ln2T_sb[:, :]), waits=ev2, sem="ldK")
                    ln2T_free = tl2
                    tl2_tok[qt] = tl2

                pend4[0] = post4
            l2t = [T(sc, f"l2t{i}", [128, D], BF16) for i in range(2)]
            g_f = T(sc, "g_f", [128, 1024], F32)
            h_f = T(sc, "h_f", [128, 1024], F32)
            a_f = T(sc, "a_f", [128, 1024], F32)
            hm = [T(sc, f"hm{i}", [128, 1024], BF16) for i in range(2)]
            mw = {nm: T(sc, f"mw_{nm}", [128, 256], F32)
                  for nm in ("w01", "w02", "w03", "w12", "w13", "w23", "t0")}
            st5 = T(sc, "st5", [128, 1024], BF16)
            psG = [T(sc, f"psG{i}", [128, 1024], F32, psum=True) for i in range(2)]
            psU = T(sc, "psU", [128, 1024], F32, psum=True)
            psT5 = T(sc, "psT5", [128, 1024], BF16, psum=True)

            wg_free = [None, None]
            l2t_free = [None, None]
            psG_free = [None, None]
            psU_free = None
            psT5_free = None
            g_free = None
            h_free = None
            a_free = None
            hm_free = [None, None]
            st5_free = None
            mask_busy = None
            pend5 = [None]
            h_st = {}

            sched = ([(0, [0, 1, 2, 3]), (1, [0, 1, 2, 3]),
                      (0, [4, 5, 6, 7]), (1, [4, 5, 6, 7])]
                     + [(i, list(range(NQT))) for i in range(2, 8)])
            tw_by_ig = {0: twg0_holder, 1: tw_ig1}
            for seg_i, (ig, rts) in enumerate(sched):
                wsl = ig % 2
                isl = slice(ig * 1024, (ig + 1) * 1024)
                if seg_i == 0:
                    for _q in range(3):
                        p4_tile(_q)
                elif seg_i == 2:
                    for _q in range(4, 8):
                        p4_tile(_q)
                    if pend4[0] is not None:
                        pend4[0]()
                        pend4[0] = None
                if ig not in tw_by_ig:
                    tw_by_ig[ig] = (
                        p.op("gpsimd", lambda e, s=isl, wsl=wsl: e.dma_start(
                            out=wgs[wsl][:, :].rearrange("p (hc c) -> p hc c", c=1024),
                            in_=wg_view[:, :, s]), waits=[wg_free[wsl]], sem=("gwB", "gwA")[wsl]),
                        p.op("gpsimd", lambda e, s=isl, wsl=wsl: e.dma_start(
                            out=wus[wsl][:, :].rearrange("p (hc c) -> p hc c", c=1024),
                            in_=wu_view[:, :, s]), waits=[], sem=("gwC", "gwD")[wsl]))
                twg, twu = tw_by_ig[ig]
                last_mm = None
                for rt in rts:
                    if seg_i == 0 and rt == 0:
                        p4_tile(3)
                    elif seg_i == 0 and rt == 1 and pend4[0] is not None:
                        pend4[0]()
                        pend4[0] = None
                    lsl = rt % 2
                    gsl = rt % 2
                    hsl = rt % 2
                    tl2 = p.op("sync", lambda e, rt=rt, lsl=lsl: e.dma_start(
                        out=l2t[lsl][:, :], in_=s_ln2T[rt, :, :]),
                        waits=[l2t_free[lsl], tl2_tok.get(rt)],
                        sem=("ldA", "ldB")[lsl])
                    tg = tu = None
                    for hc in range(NHC):
                        for half in range(2):
                            tg = p.op("tensor", lambda e, hc=hc, half=half, wsl=wsl, lsl=lsl, gsl=gsl: e.matmul(
                                psG[gsl][:, half * 512:(half + 1) * 512],
                                l2t[lsl][:, hc * 128:(hc + 1) * 128],
                                wgs[wsl][:, hc * 1024 + half * 512:hc * 1024 + (half + 1) * 512],
                                start=(hc == 0), stop=(hc == NHC - 1)),
                                waits=([tl2, twg, psG_free[gsl]] if (hc == 0 and half == 0) else []),
                                inc=(hc == NHC - 1 and half == 1))
                    for hc in range(NHC):
                        for half in range(2):
                            tu = p.op("tensor", lambda e, hc=hc, half=half, wsl=wsl, lsl=lsl: e.matmul(
                                psU[:, half * 512:(half + 1) * 512],
                                l2t[lsl][:, hc * 128:(hc + 1) * 128],
                                wus[wsl][:, hc * 1024 + half * 512:hc * 1024 + (half + 1) * 512],
                                start=(hc == 0), stop=(hc == NHC - 1)),
                                waits=([tl2, twu, psU_free] if (hc == 0 and half == 0) else []),
                                inc=(hc == NHC - 1 and half == 1))
                    last_mm = tu
                    l2t_free[lsl] = tu
                    if pend5[0] is not None:
                        pend5[0]()
                        pend5[0] = None
                    tsi = p.op("scalar", lambda e, gsl=gsl: e.activation(out=g_f[:, :], in_=psG[gsl][:, :],
                                                                         func=AF.Silu), waits=[tg, g_free])
                    psG_free[gsl] = tsi
                    th = p.op("vector", lambda e: e.tensor_tensor(h_f[:, :], g_f[:, :], psU[:, :],
                                                                  ALU.mult), waits=[tsi, tu, h_free])
                    psU_free = th
                    g_free = th
                    tab = p.op("scalar", lambda e: e.activation(out=a_f[:, :], in_=h_f[:, :],
                                                                func=AF.Abs), waits=[th, a_free])
                    qv = [a_f[:, i * 256:(i + 1) * 256] for i in range(4)]
                    pairs = [("w01", 0, 1), ("w02", 0, 2), ("w03", 0, 3),
                             ("w12", 1, 2), ("w13", 1, 3), ("w23", 2, 3)]
                    cmp_toks = {}
                    tprev = mask_busy
                    for nm, i, j in pairs:
                        cmp_toks[nm] = p.op("vector", lambda e, nm=nm, i=i, j=j: e.tensor_tensor(
                            mw[nm][:, :], qv[i], qv[j], ALU.is_ge), waits=[tab, tprev])
                        tprev = None
                    hm_toks = []

                    def mfin(expr_tok, thr, op, qidx, hsl=hsl):
                        tap = p.op("vector", lambda e, thr=thr, op=op, qidx=qidx, hsl=hsl: e.scalar_tensor_tensor(
                            hm[hsl][:, qidx * 256:(qidx + 1) * 256],
                            mw["t0"][:, :], thr, h_f[:, qidx * 256:(qidx + 1) * 256],
                            op, ALU.mult),
                            waits=[expr_tok, hm_free[hsl]] + ([] if hm_toks else []))
                        hm_toks.append(tap)
                        return tap

                    e1 = p.op("vector", lambda e: e.tensor_add(mw["t0"][:, :], mw["w01"][:, :], mw["w02"][:, :]),
                              waits=[cmp_toks["w01"], cmp_toks["w02"]])
                    e2 = p.op("vector", lambda e: e.tensor_add(mw["t0"][:, :], mw["t0"][:, :], mw["w03"][:, :]),
                              waits=[e1, cmp_toks["w03"]])
                    mfin(e2, 1.5, ALU.is_ge, 0)
                    e3 = p.op("vector", lambda e: e.tensor_add(mw["t0"][:, :], mw["w12"][:, :], mw["w13"][:, :]),
                              waits=[cmp_toks["w12"], cmp_toks["w13"], hm_toks[-1]])
                    e4 = p.op("vector", lambda e: e.tensor_sub(mw["t0"][:, :], mw["t0"][:, :], mw["w01"][:, :]),
                              waits=[e3])
                    mfin(e4, 0.5, ALU.is_ge, 1)
                    e5 = p.op("vector", lambda e: e.tensor_sub(mw["t0"][:, :], mw["w23"][:, :], mw["w02"][:, :]),
                              waits=[cmp_toks["w23"], hm_toks[-1]])
                    e6 = p.op("vector", lambda e: e.tensor_sub(mw["t0"][:, :], mw["t0"][:, :], mw["w12"][:, :]),
                              waits=[e5])
                    mfin(e6, -0.5, ALU.is_ge, 2)
                    e7 = p.op("vector", lambda e: e.tensor_add(mw["t0"][:, :], mw["w03"][:, :], mw["w13"][:, :]),
                              waits=[hm_toks[-1]])
                    e8 = p.op("vector", lambda e: e.tensor_add(mw["t0"][:, :], mw["t0"][:, :], mw["w23"][:, :]),
                              waits=[e7])
                    mfin(e8, 1.5, ALU.is_le, 3)
                    mask_busy = hm_toks[-1]
                    h_free = hm_toks[-1]
                    a_free = hm_toks[-1]
                    tmask = hm_toks[-1]

                    def post5(rt=rt, ig=ig, hsl=hsl, tmask=tmask):
                        nonlocal psT5_free, st5_free
                        tts = None
                        for k in range(8):
                            tts = p.op("tensor", lambda e, k=k, hsl=hsl: e.transpose(
                                psT5[:, k * 128:(k + 1) * 128],
                                hm[hsl][:, k * 128:(k + 1) * 128], ident[:, :]),
                                waits=([tmask, psT5_free] if k == 0 else []), inc=(k == 7))
                        hm_free[hsl] = tts
                        tev = p.op("scalar", lambda e: e.activation(out=st5[:, :], in_=psT5[:, :],
                                                                    func=AF.Copy),
                                   waits=[tts, st5_free])
                        psT5_free = tev
                        tfl = p.op("scalar", lambda e, rt=rt, ig=ig: e.dma_start(
                            out=s_hT[rt, :, ig * 8:(ig + 1) * 8, :],
                            in_=st5[:, :].rearrange("p (ic c) -> p ic c", c=128)), waits=[tev],
                            sem="ldC")
                        st5_free = tfl
                        h_st[(ig, rt)] = tfl

                    pend5[0] = post5
                if seg_i >= 2:
                    wg_free[wsl] = last_mm
            if pend5[0] is not None:
                pend5[0]()
                pend5[0] = None
            wd_view = wd_d.rearrange("(ic p) o -> p ic o", p=128)
            twd0_lo = p.op("gpsimd", lambda e: e.dma_start(
                out=wgs0[:, :].rearrange("p (ic c) -> p ic c", c=512),
                in_=wd_view[:, 0:32, 0:512]), waits=[wg_free[0]], sem="gwE")
            twd0_hi = p.op("gpsimd", lambda e: e.dma_start(
                out=wus0_hold[0][:, :].rearrange("p (ic c) -> p ic c", c=512),
                in_=wd_view[:, 32:64, 0:512]), waits=[], sem="gwF")

        p.barrier()

        with contextlib.ExitStack() as sc:
            wdsB = T(sc, "wdsB", [128, NIC * 512], BF16)
            hts = [T(sc, f"hts{i}", [128, II], BF16) for i in range(3)]
            x2t = [T(sc, f"x2t{i}", [128, 512], F32) for i in range(2)]
            outb = [T(sc, f"outb{i}", [128, 512], F32) for i in range(2)]
            psY = [T(sc, f"psY{i}", [128, 512], F32, psum=True) for i in range(2)]

            wd_free = [None, None]
            ht_free = [None, None, None]
            x2t_free = [None, None]
            out_free = [None, None]
            psY_free = [None, None]
            ty_hist = {}

            def ht_load(j):
                oc, rt = divmod(j, NQT)
                hsl = j % 3
                return p.op("sync", lambda e, rt=rt, hsl=hsl: e.dma_start(
                    out=hts[hsl][:, :].rearrange("p (ic c) -> p ic c", c=128),
                    in_=s_hT[rt, :, :, :]), waits=[ht_free[hsl]],
                    sem=("ldA", "ldB", "ldC")[hsl])

            th_tok = {0: ht_load(0), 1: ht_load(1), 2: ht_load(2)}
            for oc in range(4):
                wsl = oc % 2
                osl_cols = slice(oc * 512, (oc + 1) * 512)
                if oc == 0:
                    twd_lo, twd_hi = twd0_lo, twd0_hi
                elif oc == 2:
                    twd_lo = p.op("gpsimd", lambda e, s=osl_cols: e.dma_start(
                        out=wgs0[:, :].rearrange("p (ic c) -> p ic c", c=512),
                        in_=wd_view[:, 0:32, s]), waits=[wd_free[0]], sem="gwE")
                    twd_hi = p.op("gpsimd", lambda e, s=osl_cols: e.dma_start(
                        out=wus0_hold[0][:, :].rearrange("p (ic c) -> p ic c", c=512),
                        in_=wd_view[:, 32:64, s]), waits=[wd_free[0]], sem="gwF")
                else:
                    twd_lo = twd_hi = p.op("gpsimd", lambda e, s=osl_cols: e.dma_start(
                        out=wdsB[:, :].rearrange("p (ic c) -> p ic c", c=512),
                        in_=wd_view[:, :, s]),
                        waits=[wd_free[1]] if oc == 3 else [], sem="gwB")
                last_mm = None
                for rt in range(NQT):
                    j = oc * NQT + rt
                    hsl = j % 3
                    xsl = j % 2
                    tx2 = p.op("sync", lambda e, rt=rt, oc=oc, xsl=xsl: e.dma_start(
                        out=x2t[xsl][:, :],
                        in_=s_x2[rt * 128:(rt + 1) * 128, oc * 512:(oc + 1) * 512]),
                        waits=[x2t_free[xsl]], sem=("ldE", "ldF")[xsl])
                    ty = None
                    psl = j % 2
                    for ic in range(NIC):
                        if oc % 2 == 0:
                            wop = (wgs0[:, ic * 512:(ic + 1) * 512] if ic < 32 else
                                   wus0_hold[0][:, (ic - 32) * 512:(ic - 31) * 512])
                        else:
                            wop = wdsB[:, ic * 512:(ic + 1) * 512]
                        ty = p.op("tensor", lambda e, ic=ic, hsl=hsl, psl=psl, wop=wop: e.matmul(
                            psY[psl][:, :], hts[hsl][:, ic * 128:(ic + 1) * 128], wop,
                            start=(ic == 0), stop=(ic == NIC - 1)),
                            waits=([th_tok[j], twd_lo, psY_free[psl]]
                                   if ic == 0 else
                                   ([twd_hi] if (oc % 2 == 0 and ic == 32) else [])),
                            inc=(ic == NIC - 1))
                    last_mm = ty
                    ht_free[hsl] = ty
                    if j + 3 < 4 * NQT:
                        th_tok[j + 3] = ht_load(j + 3)
                    tev = p.op("vector", lambda e, xsl=xsl, psl=psl: e.tensor_tensor(
                        outb[xsl][:, :], psY[psl][:, :], x2t[xsl][:, :], ALU.add),
                        waits=[ty, tx2, out_free[xsl]])
                    psY_free[psl] = tev
                    x2t_free[xsl] = tev
                    tso = p.op("scalar", lambda e, rt=rt, oc=oc, xsl=xsl: e.dma_start(
                        out=out_d[rt * 128:(rt + 1) * 128, oc * 512:(oc + 1) * 512],
                        in_=outb[xsl][:, :]), waits=[tev], sem=("ldG", "ldH")[xsl])
                    out_free[xsl] = tso
                wd_free[wsl] = last_mm

        p45_scope.close()
        p.barrier()
        p.flush(block, sems)

    return nc


_NC_CACHE = {}


def kernel(L0_attn_x, L0_attn_wq, L0_attn_wk, L0_attn_wv, L0_attn_wo,
           L0_ffn_w_gate, L0_ffn_w_up, L0_ffn_w_down):
    x = np.ascontiguousarray(np.asarray(L0_attn_x, dtype=np.float32))
    wq = np.asarray(L0_attn_wq, dtype=np.float32)
    wk = np.asarray(L0_attn_wk, dtype=np.float32)
    wv = np.asarray(L0_attn_wv, dtype=np.float32)
    wo = np.asarray(L0_attn_wo, dtype=np.float32)
    wg = np.asarray(L0_ffn_w_gate, dtype=np.float32)
    wu = np.asarray(L0_ffn_w_up, dtype=np.float32)
    wd = np.asarray(L0_ffn_w_down, dtype=np.float32)

    hperm = np.concatenate([h * HD + np.concatenate([np.arange(0, HD, 2), np.arange(1, HD, 2)])
                            for h in range(H)])
    wqT = np.ascontiguousarray(wq.T[:, hperm])
    wkT = np.ascontiguousarray(wk.T[:, hperm])
    wvT = np.ascontiguousarray(wv.T)
    woT = np.ascontiguousarray(wo.T)

    iperm = np.empty(II, dtype=np.int64)
    for blk in range(II // 1024):
        base = blk * 1024
        cols = base + np.arange(1024)
        g = cols.reshape(256, 4)
        iperm[base:base + 1024] = np.concatenate([g[:, e] for e in range(4)])
    wg_p = np.ascontiguousarray(wg[:, iperm])
    wu_p = np.ascontiguousarray(wu[:, iperm])
    wd_p = np.ascontiguousarray(wd[iperm, :])

    t64 = np.arange(S, dtype=np.float64)
    inv = 1.0 / (10000.0 ** (np.arange(0, HD, 2, dtype=np.float64) / HD))
    ang = t64[:, None] * inv[None, :]
    cosT = np.cos(ang).astype(np.float32)
    sinT = np.sin(ang).astype(np.float32)
    identity = np.eye(128, dtype=ml_dtypes.bfloat16)

    key = "nc"
    if key not in _NC_CACHE:
        _NC_CACHE[key] = build_nc()
    nc = _NC_CACHE[key]

    in_maps = []
    for c in range(8):
        b, hg = c // 4, c % 4
        xb = np.ascontiguousarray(x[b])
        in_maps.append({
            "x": xb,
            "xq": np.ascontiguousarray(x[b, hg * QR:(hg + 1) * QR, :]),
            "wqT": np.ascontiguousarray(wqT[:, hg * 512:(hg + 1) * 512]),
            "wkT": np.ascontiguousarray(wkT[:, hg * 512:(hg + 1) * 512]),
            "wvT": np.ascontiguousarray(wvT[:, hg * 512:(hg + 1) * 512]),
            "woTs": np.ascontiguousarray(woT[hg * 512:(hg + 1) * 512, :]),
            "wg": wg_p, "wu": wu_p, "wd": wd_p,
            "cosT": cosT, "sinT": sinT,
            "ident": identity,
        })

    res = run_bass_kernel_spmd(nc, in_maps, core_ids=list(range(8)))

    out = np.empty((B, S, D), dtype=np.float32)
    for c in range(8):
        b, j = c // 4, c % 4
        out[b, j * QR:(j + 1) * QR, :] = res.results[c]["out"]
    return out
